# version 11
# baseline (speedup 1.0000x reference)
import numpy as np
import ml_dtypes
import concourse.bass as bass
import concourse.mybir as mybir
from concourse.bass_utils import run_bass_kernel_spmd

F32 = mybir.dt.float32
BF16 = mybir.dt.bfloat16
I32 = mybir.dt.int32
AF = mybir.ActivationFunctionType
ALU = mybir.AluOpType

D = 4096
NCH = 32
HD = 128
EPS = 1e-5
SOFTMAX_B = 30.0
NEG = -1e9

ENGINES = ("tensor", "vector", "scalar", "gpsimd", "sync")

bfdt = ml_dtypes.bfloat16

SAME_ENGINE_SAFE = {"tensor"}



class Prog:

    def __init__(self):
        self.ops = {e: [] for e in ENGINES}
        self.ctr = {e: 0 for e in ENGINES}
        self.sem_names = ["e_" + e for e in ENGINES]
        self.dma_sems = {}
        self.last_write = {}
        self.readers = {}
        self.floors = {e: {} for e in ENGINES}
        self.n_waits = 0
        self.barrier_need = {}

    def dma_sem(self, name):
        name = "d_" + name
        if name not in self.dma_sems:
            self.dma_sems[name] = 0
            self.sem_names.append(name)
        return name

    def _waits_for(self, eng, reads, writes):
        need = {}
        for k in reads:
            t = self.last_write.get(k)
            if t:
                need[t[0]] = max(need.get(t[0], 0), t[1])
        for k in writes:
            t = self.last_write.get(k)
            if t:
                need[t[0]] = max(need.get(t[0], 0), t[1])
            for t in self.readers.get(k, ()):
                need[t[0]] = max(need.get(t[0], 0), t[1])
        for s, v in self.barrier_need.items():
            need[s] = max(need.get(s, 0), v)
        fl = self.floors[eng]
        own = "e_" + eng
        out = []
        for s, v in sorted(need.items()):
            if s == own and eng in SAME_ENGINE_SAFE:
                continue
            if fl.get(s, 0) >= v:
                continue
            out.append((s, v))
            fl[s] = v
        self.n_waits += len(out)
        return out

    def _record(self, tok, reads, writes):
        for k in writes:
            self.last_write[k] = tok
            self.readers[k] = []
        for k in reads:
            self.readers.setdefault(k, []).append(tok)

    def op(self, eng, fn, reads=(), writes=()):
        waits = self._waits_for(eng, reads, writes)
        self.ctr[eng] += 1
        tok = ("e_" + eng, self.ctr[eng])
        self._record(tok, reads, writes)

        def closure(E, sems, waits=tuple(waits), fn=fn, tok=tok):
            for s, v in waits:
                E.wait_ge(sems[s], v)
            inst = fn(E, sems)
            inst.then_inc(sems[tok[0]], 1)
        self.ops[eng].append(closure)
        return tok

    def dma(self, q, slot, fn, reads=(), writes=()):
        sname = self.dma_sem(slot)
        waits = self._waits_for(q, reads, writes)
        pk = ("__slot__", sname)
        t = self.last_write.get(pk)
        if t and self.floors[q].get(t[0], 0) < t[1]:
            waits.append((t[0], t[1]))
            self.floors[q][t[0]] = t[1]
        self.dma_sems[sname] += 16
        tok = (sname, self.dma_sems[sname])
        self._record(tok, reads, writes)
        self.last_write[pk] = tok
        self.readers[pk] = []

        def closure(E, sems, waits=tuple(waits), fn=fn, tok=tok):
            for s, v in waits:
                E.wait_ge(sems[s], v)
            inst = fn(E, sems)
            inst.then_inc(sems[tok[0]], 16)
        self.ops[q].append(closure)
        return tok

    def barrier(self):
        bn = {}
        for e in ENGINES:
            if self.ctr[e]:
                bn["e_" + e] = self.ctr[e]
        for sname, v in self.dma_sems.items():
            if v:
                bn[sname] = v
        self.barrier_need = bn

    def finalize(self, nc, out_dma_slots):
        sems = {}
        ctxs = []
        for n in self.sem_names:
            cm = nc.semaphore(n)
            sems[n] = cm.__enter__()
            ctxs.append(cm)
        blk = nc.Block()
        block = blk.__enter__()
        prog = self

        def make(engname):
            def body(E):
                for cl in prog.ops[engname]:
                    cl(E, sems)
                if engname == "sync":
                    for sl in out_dma_slots:
                        sn = "d_" + sl
                        E.wait_ge(sems[sn], prog.dma_sems[sn])
            return body

        for engname in ENGINES:
            getattr(block, engname)(make(engname))
        blk.__exit__(None, None, None)
        for cm in reversed(ctxs):
            cm.__exit__(None, None, None)


class Arena:

    def __init__(self, nc, ctxs, bf16_cols=70 * 1024, f32_cols=13 * 1024 + 512,
                 i32_cols=2 * 1024):
        self.tensors = {}
        for name, dt, cols in (("bf16", BF16, bf16_cols), ("f32", F32, f32_cols),
                               ("i32", I32, i32_cols)):
            cm = nc.sbuf_tensor("arena_" + name, [128, cols], dt)
            self.tensors[name] = cm.__enter__()
            ctxs.append(cm)
            setattr(self, "cap_" + name, cols)
        self.off = {"bf16": 0, "f32": 0, "i32": 0}
        self.marks = []

    def alloc(self, kind, cols, parts=128):
        o = self.off[kind]
        assert o + cols <= getattr(self, "cap_" + kind), \
            f"arena {kind} overflow {o}+{cols}"
        self.off[kind] = o + cols
        return self.tensors[kind][0:parts, o:o + cols]

    def push(self):
        self.marks.append(dict(self.off))

    def pop(self):
        self.off = self.marks.pop()


def bank_key(i):
    if isinstance(i, tuple):
        i = i[0]
    return ("bank", i)


def half_bank(P, b):
    i, h = b
    return P.psum[i][:, h * 512:(h + 1) * 512]



def emit_layernorm(P, ar, x_dram, xn_dram, T, tag, psum, consts, xkey, xnkey):
    nt = T // 512
    ar.push()
    xt = [ar.alloc("f32", 512) for _ in range(2)]
    xsq = [ar.alloc("bf16", 512) for _ in range(2)]
    mrow = ar.alloc("f32", 512, parts=1)
    vrow = ar.alloc("f32", 512, parts=1)
    trow = ar.alloc("f32", 512, parts=1)
    rstd = ar.alloc("f32", 512)
    xnt = [ar.alloc("bf16", 512) for _ in range(2)]
    (iS, iQ, iM, iL) = psum
    pS, pQ, pM, pL = (half_bank(P, b) for b in psum)
    ones_f = consts["ones_f"]
    onesrow_f = consts["onesrow_f"]
    neghalf_f = consts["neghalf_f"]
    ones_b = consts["ones_b"]

    for tb in range(nt):
        t0 = tb * 512
        for c in range(NCH):
            buf, kx = xt[c % 2], (tag, "xt", c % 2)
            sq, ksq = xsq[c % 2], (tag, "xsq", c % 2)
            P.dma("sync", "lnx%d" % (c % 2),
                  lambda E, S, buf=buf, c=c, t0=t0: E.dma_start(
                      out=buf, in_=x_dram[c * 128:(c + 1) * 128, t0:t0 + 512]),
                  reads=[xkey(c, tb)], writes=[kx])
            P.op("tensor",
                 lambda E, S, buf=buf, c=c: E.matmul(
                     pS[0:1, :], ones_f, buf, start=(c == 0), stop=(c == NCH - 1)),
                 reads=[kx, ("c", "ones_f")], writes=[bank_key(iS)])
            P.op("scalar",
                 lambda E, S, buf=buf, sq=sq: E.activation(
                     out=sq, in_=buf, func=AF.Square),
                 reads=[kx], writes=[ksq])
            P.op("tensor",
                 lambda E, S, sq=sq, c=c: E.matmul(
                     pQ[0:1, :], ones_b, sq, start=(c == 0), stop=(c == NCH - 1)),
                 reads=[ksq, ("c", "ones_b")], writes=[bank_key(iQ)])
        P.op("vector",
             lambda E, S: E.tensor_scalar(out=mrow, in0=pS[0:1, :],
                                          scalar1=1.0 / D, scalar2=None,
                                          op0=ALU.mult),
             reads=[bank_key(iS)], writes=[(tag, "mrow")])
        P.op("vector",
             lambda E, S: E.tensor_scalar(out=vrow, in0=pQ[0:1, :],
                                          scalar1=1.0 / D, scalar2=EPS,
                                          op0=ALU.mult, op1=ALU.add),
             reads=[bank_key(iQ)], writes=[(tag, "vrow")])
        P.op("vector",
             lambda E, S: E.tensor_tensor(out=trow, in0=mrow, in1=mrow,
                                          op=ALU.mult),
             reads=[(tag, "mrow")], writes=[(tag, "trow")])
        P.op("vector",
             lambda E, S: E.tensor_tensor(out=vrow, in0=vrow, in1=trow,
                                          op=ALU.subtract),
             reads=[(tag, "vrow"), (tag, "trow")], writes=[(tag, "vrow")])
        P.op("scalar",
             lambda E, S: E.activation(out=trow, in_=vrow, func=AF.Ln),
             reads=[(tag, "vrow")], writes=[(tag, "trow")])
        P.op("tensor",
             lambda E, S: E.matmul(pM, onesrow_f, mrow, start=True, stop=True),
             reads=[(tag, "mrow"), ("c", "onesrow_f")], writes=[bank_key(iM)])
        P.op("tensor",
             lambda E, S: E.matmul(pL, neghalf_f, trow, start=True, stop=True),
             reads=[(tag, "trow"), ("c", "neghalf_f")], writes=[bank_key(iL)])
        P.op("scalar",
             lambda E, S: E.activation(out=rstd, in_=pL, func=AF.Exp),
             reads=[bank_key(iL)], writes=[(tag, "rstd")])
        for c in range(NCH):
            buf, kx = xt[c % 2], (tag, "xt", c % 2)
            ot, ko = xnt[c % 2], (tag, "xnt", c % 2)
            P.dma("sync", "lnx%d" % (c % 2),
                  lambda E, S, buf=buf, c=c, t0=t0: E.dma_start(
                      out=buf, in_=x_dram[c * 128:(c + 1) * 128, t0:t0 + 512]),
                  reads=[xkey(c, tb)], writes=[kx])
            P.op("vector",
                 lambda E, S, buf=buf: E.tensor_tensor(
                     out=buf, in0=buf, in1=pM, op=ALU.subtract),
                 reads=[kx, bank_key(iM)], writes=[kx])
            P.op("vector",
                 lambda E, S, buf=buf, ot=ot: E.tensor_tensor(
                     out=ot, in0=buf, in1=rstd, op=ALU.mult),
                 reads=[kx, (tag, "rstd")], writes=[ko])
            P.dma("sync", "lno%d" % (c % 2),
                  lambda E, S, ot=ot, c=c, t0=t0: E.dma_start(
                      out=xn_dram[c * 128:(c + 1) * 128, t0:t0 + 512], in_=ot),
                  reads=[ko], writes=[xnkey(c, tb)])
    ar.pop()


def make_gemm_fm_gen(P, ar, tag, w_dram, Dout, rhs_dram, T, banks, epilogue,
                     consts, tok0=0, w_is_f32=True, osub=1024, extra_mms=None,
                     rhs_key=None, w_key=None, nslab=2, slabq="sync"):
    nt = T // 512
    nsub = (Dout + osub - 1) // osub
    wsb = ar.alloc("bf16", NCH * osub)
    slab = [ar.alloc("bf16", NCH * 512) for _ in range(nslab)]
    if rhs_key is None:
        rhs_key = lambda: (tag, "rhs_dram")
    if w_key is None:
        w_key = lambda: (tag, "w_dram")
    rr = rhs_dram.rearrange("(c p) t -> p c t", p=128)

    def gen():
        for sub in range(nsub):
            o0 = sub * osub
            ow = min(osub, Dout - o0)
            for c in range(NCH):
                q = "gpsimd" if w_is_f32 else "sync"
                P.dma(q, "gw%d" % (c % 2),
                      lambda E, S, c=c, o0=o0, ow=ow: E.dma_start(
                          out=wsb[:, c * osub:c * osub + ow],
                          in_=w_dram[c * 128:(c + 1) * 128, o0:o0 + ow]),
                      reads=[w_key()], writes=[(tag, "w", c)])
            for tb in range(nt):
                sl = slab[tb % nslab]
                ksl = (tag, "slab", tb % nslab)
                t0 = tok0 + tb * 512
                for piece in range(4):
                    P.dma(slabq, "gs%d_%d" % (tb % nslab, piece),
                          lambda E, S, sl=sl, piece=piece, t0=t0: E.dma_start(
                              out=sl[:, piece * 8 * 512:(piece + 1) * 8 * 512],
                              in_=rr[:, piece * 8:(piece + 1) * 8, t0:t0 + 512]),
                          reads=[rhs_key()], writes=[(ksl, piece)])
                slab_keys = [(ksl, p) for p in range(4)]
                for ot in range(ow // 128):
                    og = (o0 + ot * 128) // 128
                    bi = banks[(sub * nt * (osub // 128) + tb * (ow // 128) + ot) % 2]
                    pb = half_bank(P, bi)
                    for c in range(NCH):
                        last = (c == NCH - 1) and extra_mms is None
                        P.op("tensor",
                             lambda E, S, c=c, ot=ot, pb=pb, last=last, sl=sl: E.matmul(
                                 pb, wsb[:, c * osub + ot * 128:c * osub + ot * 128 + 128],
                                 sl[:, c * 512:(c + 1) * 512],
                                 start=(c == 0), stop=last),
                             reads=[(tag, "w", c), slab_keys[c // 8]],
                             writes=[bank_key(bi)])
                    if extra_mms is not None:
                        extra_mms(og, tb, bi, pb)
                    epilogue(og, tb, bi, pb)
                    yield
    return gen()


def emit_gemm_fm(P, ar, tag, w_dram, Dout, rhs_dram, T, banks, epilogue,
                 consts, tok0=0, w_is_f32=True, osub=1024, extra_mms=None,
                 rhs_key=None, w_key=None):
    ar.push()
    g = make_gemm_fm_gen(P, ar, tag, w_dram, Dout, rhs_dram, T, banks,
                         epilogue, consts, tok0, w_is_f32, osub, extra_mms,
                         rhs_key, w_key)
    for _ in g:
        pass
    ar.pop()


def emit_rope_epilogue(P, ar, tag, out_dram, consts, tok0, out_key=None):
    ev = [ar.alloc("f32", 512) for _ in range(2)]
    t1 = [ar.alloc("f32", 512) for _ in range(2)]
    t2 = [ar.alloc("f32", 512) for _ in range(2)]
    ro = [ar.alloc("bf16", 512) for _ in range(2)]
    cos_t = consts["cos_t"]
    ss = consts["sinsign"]
    ctr = [0]

    def epi(og, tb, bi, pb):
        i = ctr[0] % 2
        ctr[0] += 1
        t0 = tok0 + tb * 512
        kev, kt1, kt2, kro = (tag, "ev", i), (tag, "t1", i), (tag, "t2", i), (tag, "ro", i)
        P.op("scalar",
             lambda E, S, i=i, pb=pb: E.activation(out=ev[i], in_=pb, func=AF.Copy),
             reads=[bank_key(bi)], writes=[kev])
        P.op("vector",
             lambda E, S, i=i, t0=t0: E.tensor_tensor(
                 out=t1[i], in0=ev[i], in1=cos_t[:, t0:t0 + 512], op=ALU.mult),
             reads=[kev, ("c", "cos_t")], writes=[kt1])
        P.op("vector",
             lambda E, S, i=i, pb=pb, t0=t0: E.tensor_tensor(
                 out=t2[i][0:64, :], in0=pb[64:128, :],
                 in1=ss[0:64, t0:t0 + 512], op=ALU.mult),
             reads=[bank_key(bi), ("c", "sinsign")], writes=[(kt2, "lo")])
        P.op("vector",
             lambda E, S, i=i, pb=pb, t0=t0: E.tensor_tensor(
                 out=t2[i][64:128, :], in0=pb[0:64, :],
                 in1=ss[64:128, t0:t0 + 512], op=ALU.mult),
             reads=[bank_key(bi), ("c", "sinsign")], writes=[(kt2, "hi")])
        P.op("vector",
             lambda E, S, i=i: E.tensor_tensor(
                 out=ro[i], in0=t1[i], in1=t2[i], op=ALU.add),
             reads=[kt1, (kt2, "lo"), (kt2, "hi")], writes=[kro])
        P.dma("sync", "go%d" % i,
              lambda E, S, i=i, og=og, tb=tb: E.dma_start(
                  out=out_dram[og * 128:(og + 1) * 128, tb * 512:(tb + 1) * 512],
                  in_=ro[i]),
              reads=[kro],
              writes=[out_key(og, tb) if out_key else (tag, "out", og, tb)])
    return epi


def emit_copy_bf16_epilogue(P, ar, tag, out_dram, out_key=None):
    ob = [ar.alloc("bf16", 512) for _ in range(2)]
    ctr = [0]

    def epi(og, tb, bi, pb):
        i = ctr[0] % 2
        ctr[0] += 1
        ko = (tag, "ob", i)
        P.op("scalar",
             lambda E, S, i=i, pb=pb: E.activation(out=ob[i], in_=pb, func=AF.Copy),
             reads=[bank_key(bi)], writes=[ko])
        P.dma("sync", "go%d" % i,
              lambda E, S, i=i, og=og, tb=tb: E.dma_start(
                  out=out_dram[og * 128:(og + 1) * 128, tb * 512:(tb + 1) * 512],
                  in_=ob[i]),
              reads=[ko],
              writes=[out_key(og, tb) if out_key else (tag, "out", og, tb)])
    return epi


def emit_gemm_tm_v(P, ar, tag, w_dram, Dout, rhs_dram, T, banks, v_dram,
                   consts, tok0=0, rhs_key=None, out_key=None,
                   part_major=False, col0=0):
    nt = T // 512
    nsub = Dout // 512
    ar.push()
    wsb = ar.alloc("bf16", NCH * 512)
    slab = [ar.alloc("bf16", NCH * 512) for _ in range(2)]
    vo = [ar.alloc("bf16", 512) for _ in range(2)]
    if rhs_key is None:
        rhs_key = lambda: (tag, "rhs_dram")
    rr = rhs_dram.rearrange("(c p) t -> p c t", p=128)
    vv = None if part_major else v_dram.rearrange("h t d -> t h d")
    ctr = [0]
    for sub in range(nsub):
        o0 = sub * 512
        for c in range(NCH):
            P.dma("gpsimd", "gw%d" % (c % 2),
                  lambda E, S, c=c, o0=o0: E.dma_start(
                      out=wsb[:, c * 512:(c + 1) * 512],
                      in_=w_dram[c * 128:(c + 1) * 128, o0:o0 + 512]),
                  reads=[(tag, "w_dram")], writes=[(tag, "w", c)])
        for tb in range(nt):
            sl = slab[tb % 2]
            ksl = (tag, "slab", tb % 2)
            t0 = tok0 + tb * 512
            for piece in range(4):
                P.dma("sync", "gs%d_%d" % (tb % 2, piece),
                      lambda E, S, sl=sl, piece=piece, t0=t0: E.dma_start(
                          out=sl[:, piece * 8 * 512:(piece + 1) * 8 * 512],
                          in_=rr[:, piece * 8:(piece + 1) * 8, t0:t0 + 512]),
                      reads=[rhs_key()], writes=[(ksl, piece)])
            slab_keys = [(ksl, p) for p in range(4)]
            for tt in range(4):
                bi = banks[ctr[0] % 2]
                i = ctr[0] % 2
                ctr[0] += 1
                pb = half_bank(P, bi)
                for c in range(NCH):
                    P.op("tensor",
                         lambda E, S, c=c, tt=tt, pb=pb, sl=sl: E.matmul(
                             pb,
                             sl[:, c * 512 + tt * 128:c * 512 + tt * 128 + 128],
                             wsb[:, c * 512:(c + 1) * 512],
                             start=(c == 0), stop=(c == NCH - 1)),
                         reads=[(tag, "w", c), slab_keys[c // 8]],
                         writes=[bank_key(bi)])
                ko = (tag, "vo", i)
                P.op("scalar",
                     lambda E, S, i=i, pb=pb: E.activation(
                         out=vo[i], in_=pb, func=AF.Copy),
                     reads=[bank_key(bi)], writes=[ko])
                tg = tb * 512 + tt * 128
                if not part_major:
                    P.dma("sync", "go%d" % i,
                          lambda E, S, i=i, tg=tg, sub=sub: E.dma_start(
                              out=vv[tg:tg + 128, sub * 4:(sub + 1) * 4, :], in_=vo[i]),
                          reads=[ko],
                          writes=[out_key(sub, tg) if out_key else (tag, "vout", sub, tg)])
                else:
                    for hl in range(4):
                        kv = sub * 4 + hl
                        P.dma("sync", "go%d" % i,
                              lambda E, S, i=i, tg=tg, kv=kv, hl=hl: E.dma_start(
                                  out=v_dram[kv * 128:(kv + 1) * 128,
                                             col0 + tg:col0 + tg + 128],
                                  in_=vo[i][:, hl * 128:(hl + 1) * 128]),
                              reads=[ko],
                              writes=[(tag, "vout", kv, tg)])
    ar.pop()


def emit_attention2(P, ar, tag, q_dram, k_dram, v_dram, o_dram, Tq, Tk,
                    nheads, grp, consts, mask_ext, visits, co=None, pulls=0,
                    nkv_buf=2, co_every=None, mask_w=1024, nprobs=3, nao=2):
    nts = Tq // 1024
    ar.push()
    masksb = {}
    for mid, ext in mask_ext.items():
        msb = ar.alloc("bf16", ext.shape[-1] if hasattr(ext, "shape") else mask_w)
        masksb[mid] = msb
        P.dma("gpsimd", "constsb",
              lambda E, S, msb=msb, ext=ext: E.dma_start(out=msb, in_=ext[:, :]),
              writes=[("c", "mask", mid)])
    qsl = [ar.alloc("bf16", 1024) for _ in range(2)]
    ksl = [ar.alloc("bf16", Tk) for _ in range(nkv_buf)]
    vsl = [ar.alloc("bf16", Tk) for _ in range(nkv_buf)]
    probs = [ar.alloc("bf16", 1024) for _ in range(nprobs)]
    acc = [ar.alloc("f32", 1024) for _ in range(2)]
    lnrow = ar.alloc("f32", 1024, parts=1)
    rbc = ar.alloc("f32", 1024)
    ao = [ar.alloc("bf16", 1024) for _ in range(nao)]
    pdS = (0, 1)
    iPA = 2
    pa = P.psum[iPA]
    ones_f = consts["ones_f"]
    negrow_f = consts["negrow_f"]
    negB = consts["negB"]
    sm = HD ** -0.5
    pctr = [0]
    scctr = [0]
    qctr = [0]
    aoctr = [0]

    vctr = [0]
    for h in range(nheads):
        kvh = h // grp
        kvb = kvh % nkv_buf
        kk = (tag, "ksl", kvb)
        kv = (tag, "vsl", kvb)
        if h % grp == 0:
            P.dma("sync", "ak%d" % kvb,
                  lambda E, S, kvh=kvh, kvb=kvb: E.dma_start(
                      out=ksl[kvb], in_=k_dram[kvh * 128:(kvh + 1) * 128, 0:Tk]),
                  reads=[(tag, "k_dram")], writes=[kk])
            P.dma("sync", "av%d" % kvb,
                  lambda E, S, kvh=kvh, kvb=kvb: E.dma_start(
                      out=vsl[kvb], in_=v_dram[kvh * 128:(kvh + 1) * 128, 0:Tk]),
                  reads=[(tag, "v_dram")], writes=[kv])
        for ts in range(nts):
            qb = qctr[0] % 2
            qctr[0] += 1
            kq = (tag, "qsl", qb)
            P.dma("sync", "aq%d" % qb,
                  lambda E, S, h=h, ts=ts, qb=qb: E.dma_start(
                      out=qsl[qb], in_=q_dram[h * 128:(h + 1) * 128,
                                              ts * 1024:(ts + 1) * 1024]),
                  reads=[(tag, "q_dram")], writes=[kq])
            sts = visits[ts]
            nst = len(sts)
            par = (h * nts + ts) % 2
            ka = (tag, "acc", par)

            def emit_sc(idx):
                st, mid = sts[idx]
                si = pdS[scctr[0] % 2]
                scctr[0] += 1
                psc = P.psum[si]
                for hf in range(2):
                    P.op("tensor",
                         lambda E, S, st=st, psc=psc, kvb=kvb, qb=qb, hf=hf: E.matmul(
                             psc[:, hf * 512:(hf + 1) * 512],
                             ksl[kvb][:, st * 128:(st + 1) * 128],
                             qsl[qb][:, hf * 512:(hf + 1) * 512],
                             start=True, stop=True),
                         reads=[kk, kq], writes=[bank_key(si)])
                return si

            def emit_probs(idx, si):
                st, mid = sts[idx]
                psc = P.psum[si]
                pi = pctr[0] % nprobs
                pctr[0] += 1
                kp = (tag, "probs", pi)
                P.op("scalar",
                     lambda E, S, pi=pi, psc=psc: E.activation(
                         out=probs[pi], in_=psc[:, :], func=AF.Exp,
                         scale=sm, bias=negB),
                     reads=[bank_key(si), ("c", "negB")], writes=[kp])
                if mid is not None:
                    if isinstance(mid, tuple) and mid[0] == "__half__":
                        _, mkey, mhalf = mid
                        P.op("vector",
                             lambda E, S, pi=pi, mkey=mkey, mhalf=mhalf: E.tensor_tensor(
                                 out=probs[pi][:, mhalf * 512:(mhalf + 1) * 512],
                                 in0=probs[pi][:, mhalf * 512:(mhalf + 1) * 512],
                                 in1=masksb[mkey], op=ALU.mult),
                             reads=[kp, ("c", "mask", mkey)], writes=[kp])
                    else:
                        P.op("vector",
                             lambda E, S, pi=pi, mid=mid: E.tensor_tensor(
                                 out=probs[pi], in0=probs[pi], in1=masksb[mid],
                                 op=ALU.mult),
                             reads=[kp, ("c", "mask", mid)], writes=[kp])
                return pi

            def emit_cons(idx, pi):
                st, mid = sts[idx]
                kp = (tag, "probs", pi)
                if idx == 0:
                    P.op("vector",
                         lambda E, S, pi=pi, par=par: E.tensor_copy(
                             out=acc[par], in_=probs[pi]),
                         reads=[kp], writes=[ka])
                else:
                    P.op("vector",
                         lambda E, S, pi=pi, par=par: E.tensor_tensor(
                             out=acc[par], in0=acc[par], in1=probs[pi], op=ALU.add),
                         reads=[kp, ka], writes=[ka])
                for hf in range(2):
                    P.op("tensor",
                         lambda E, S, st=st, pi=pi, idx=idx, nst=nst, kvb=kvb, hf=hf: E.matmul(
                             pa[:, hf * 512:(hf + 1) * 512],
                             vsl[kvb][:, st * 128:(st + 1) * 128],
                             probs[pi][:, hf * 512:(hf + 1) * 512],
                             start=(idx == 0), stop=(idx == nst - 1)),
                         reads=[kv, kp],
                         writes=[bank_key(iPA)])
                vctr[0] += 1
                if co is not None and co_every and vctr[0] % co_every == 0:
                    next(co, None)

            si_prev = emit_sc(0)
            pi_prev = emit_probs(0, si_prev)
            for idx in range(1, nst):
                si = emit_sc(idx)
                emit_cons(idx - 1, pi_prev)
                pi_prev = emit_probs(idx, si)
            emit_cons(nst - 1, pi_prev)
            for hf in range(2):
                P.op("tensor",
                     lambda E, S, par=par, hf=hf: E.matmul(
                         P.psum[0][0:1, hf * 512:(hf + 1) * 512], ones_f,
                         acc[par][:, hf * 512:(hf + 1) * 512],
                         start=True, stop=True),
                     reads=[ka, ("c", "ones_f")],
                     writes=[bank_key(0)])
            P.op("scalar",
                 lambda E, S: E.activation(out=lnrow, in_=P.psum[0][0:1, :],
                                           func=AF.Ln),
                 reads=[bank_key(0)], writes=[(tag, "lnrow")])
            for hf in range(2):
                P.op("tensor",
                     lambda E, S, hf=hf: E.matmul(
                         P.psum[1][:, hf * 512:(hf + 1) * 512], negrow_f,
                         lnrow[0:1, hf * 512:(hf + 1) * 512],
                         start=True, stop=True),
                     reads=[(tag, "lnrow"), ("c", "negrow_f")],
                     writes=[bank_key(1)])
            P.op("scalar",
                 lambda E, S: E.activation(out=rbc, in_=P.psum[1][:, :], func=AF.Exp),
                 reads=[bank_key(1)], writes=[(tag, "rbc")])
            ai = aoctr[0] % nao
            aoctr[0] += 1
            kao = (tag, "ao", ai)
            P.op("vector",
                 lambda E, S, ai=ai: E.tensor_tensor(
                     out=ao[ai], in0=pa[:, :], in1=rbc, op=ALU.mult),
                 reads=[bank_key(iPA), (tag, "rbc")], writes=[kao])
            P.dma("sync", "aao%d" % ai,
                  lambda E, S, h=h, ts=ts, ai=ai: E.dma_start(
                      out=o_dram[h * 128:(h + 1) * 128, ts * 1024:(ts + 1) * 1024],
                      in_=ao[ai]),
                  reads=[kao], writes=[(tag, "attnT", h, ts)])
            if co is not None:
                for _ in range(pulls):
                    next(co, None)
    if co is not None:
        for _ in co:
            pass
    ar.pop()


def build_visits_l0(tsup=1024):
    visits = {}
    masks = {}
    for ts in range(4096 // tsup):
        lst = []
        for st in range(4096 // 128):
            gs = st * 128 + np.arange(128)[:, None]
            gt = ts * tsup + np.arange(tsup)[None, :]
            m = gs <= gt
            if not m.any():
                continue
            if m.all():
                lst.append((st, None))
            else:
                half = 0 if (st * 128) < (ts * tsup + 512) else 1
                if half == 0:
                    mh = m[:, 0:512]
                    kb = mh.tobytes()
                    masks.setdefault(kb, mh)
                    lst.append((st, ("__half__", kb, 0)))
                else:
                    kb = m.tobytes()
                    masks.setdefault(kb, m)
                    lst.append((st, kb))
        visits[ts] = lst
    return visits, masks


def l1_mask_arrays2(quarter):
    tok0 = quarter * 1024
    out = {}
    for st in range(16):
        gs = tok0 - 1024 + st * 128 + np.arange(128)[:, None]
        gt = tok0 + np.arange(1024)[None, :]
        allowed = (gs >= 0) & (gs <= gt) & (gs > gt - 1024)
        out[st] = allowed.astype(np.float32)
    return out


def emit_silu_epilogue(P, ar, tag, g_dram, bias_sb, out_key=None):
    gt = [ar.alloc("f32", 512) for _ in range(2)]
    ctr = [0]

    def epi(og, tb, bi, pb):
        i = ctr[0] % 2
        ctr[0] += 1
        ko = (tag, "gt", i)
        P.op("vector",
             lambda E, S, i=i, pb=pb: E.tensor_copy(out=gt[i], in_=pb),
             reads=[bank_key(bi)], writes=[ko])
        P.dma("sync", "go%d" % i,
              lambda E, S, i=i, og=og, tb=tb: E.dma_start(
                  out=g_dram[og * 128:(og + 1) * 128, tb * 512:(tb + 1) * 512],
                  in_=gt[i]),
              reads=[ko],
              writes=[out_key(og, tb) if out_key else (tag, "g", og, tb)])
    return epi


def emit_scale_epilogue(P, ar, tag, out_dram, wvec_sb, out_key=None):
    ot_ = [ar.alloc("bf16", 512) for _ in range(2)]
    ctr = [0]

    def epi(og, tb, bi, pb):
        i = ctr[0] % 2
        ctr[0] += 1
        ko = (tag, "xa", i)
        P.op("scalar",
             lambda E, S, i=i, pb=pb, og=og: E.activation(
                 out=ot_[i], in_=pb, func=AF.Copy,
                 scale=wvec_sb[:, og:og + 1]),
             reads=[bank_key(bi), ("c", "wvec")], writes=[ko])
        P.dma("sync", "go%d" % i,
              lambda E, S, i=i, og=og, tb=tb: E.dma_start(
                  out=out_dram[og * 128:(og + 1) * 128, tb * 512:(tb + 1) * 512],
                  in_=ot_[i]),
              reads=[ko],
              writes=[out_key(og, tb) if out_key else (tag, "out", og, tb)])
    return epi


def emit_resid_epilogue(P, ar, tag, g_dram, x_src, x_dst, gkey, xkey, out_key,
                        bias_sb, tok0=0):
    gtile = [ar.alloc("f32", 512) for _ in range(2)]
    xtile = [ar.alloc("f32", 512) for _ in range(2)]
    ut = [ar.alloc("f32", 512) for _ in range(2)]
    ctr = [0]

    def epi(og, tb, bi, pb):
        i = ctr[0] % 2
        ctr[0] += 1
        t0 = tok0 + tb * 512
        kg, kx, ku = (tag, "gtile", i), (tag, "xtile", i), (tag, "ut", i)
        P.dma("sync", "rg%d" % i,
              lambda E, S, i=i, og=og, t0=t0: E.dma_start(
                  out=gtile[i], in_=g_dram[og * 128:(og + 1) * 128, t0:t0 + 512]),
              reads=[gkey(og, tb)], writes=[kg])
        P.dma("sync", "rx%d" % i,
              lambda E, S, i=i, og=og, t0=t0: E.dma_start(
                  out=xtile[i], in_=x_src[og * 128:(og + 1) * 128, t0:t0 + 512]),
              reads=[xkey(og, tb)], writes=[kx])
        P.op("scalar",
             lambda E, S, i=i, og=og: E.activation(
                 out=gtile[i], in_=gtile[i], func=AF.Silu,
                 bias=bias_sb[:, og:og + 1]),
             reads=[kg, ("c", "bias")], writes=[kg])
        P.op("vector",
             lambda E, S, i=i, pb=pb: E.tensor_tensor(
                 out=ut[i], in0=gtile[i], in1=pb, op=ALU.mult),
             reads=[kg, bank_key(bi)], writes=[ku])
        P.op("vector",
             lambda E, S, i=i: E.tensor_tensor(
                 out=ut[i], in0=ut[i], in1=xtile[i], op=ALU.add),
             reads=[ku, kx], writes=[ku])
        P.dma("sync", "rr%d" % i,
              lambda E, S, i=i, og=og, t0=t0: E.dma_start(
                  out=x_dst[og * 128:(og + 1) * 128, t0:t0 + 512], in_=ut[i]),
              reads=[ku], writes=[out_key(og, tb)])
    return epi


def emit_dequant(P, ar, tag, packed_dram, scale_dram, w_out, consts):
    ar.push()
    psb = [ar.alloc("i32", 1024)]
    nib = ar.alloc("i32", 1024)
    ssb = [ar.alloc("f32", 1024)]
    ob = [ar.alloc("bf16", 1024) for _ in range(2)]
    for pc in range(4):
        for half in range(4):
            i = 0
            o0 = half * 1024
            kp, ks = (tag, "psb", i), (tag, "ssb", i)
            P.dma("sync", "dqp",
                  lambda E, S, pc=pc, o0=o0, i=i: E.dma_start(
                      out=psb[i], in_=packed_dram[pc * 128:(pc + 1) * 128,
                                                  o0:o0 + 1024]),
                  reads=[(tag, "packed")], writes=[kp])
            P.dma("sync", "dqs",
                  lambda E, S, pc=pc, o0=o0, i=i: E.dma_start(
                      out=ssb[i], in_=scale_dram[pc * 128:(pc + 1) * 128,
                                                 o0:o0 + 1024]),
                  reads=[(tag, "scales")], writes=[ks])
            for e in range(8):
                kn = (tag, "nib")
                ko = (tag, "ob", e % 2)
                P.op("vector",
                     lambda E, S, e=e, i=i: E.tensor_scalar(
                         out=nib, in0=psb[i], scalar1=4 * e, scalar2=28,
                         op0=ALU.logical_shift_left, op1=ALU.arith_shift_right),
                     reads=[kp], writes=[kn])
                P.op("vector",
                     lambda E, S, e=e, i=i: E.tensor_tensor(
                         out=ob[e % 2], in0=nib, in1=ssb[i], op=ALU.mult),
                     reads=[kn, ks], writes=[ko])
                wo = w_out.rearrange("(r eight) o -> r eight o", eight=8)
                P.dma("scalar", "dqo%d" % (e % 2),
                      lambda E, S, e=e, pc=pc, o0=o0, wo=wo: E.dma_start(
                          out=wo[pc * 128:(pc + 1) * 128, e, o0:o0 + 1024],
                          in_=ob[e % 2]),
                      reads=[ko], writes=[(tag, "wout", pc, e, o0)])
    ar.pop()


def make_ffn_gen(P, ar, tag, w_dram, rhs_dram, T, banks, bias_sb, g_dram,
                 consts, tok0=0, lora=None, w_is_f32=True, gkey=None,
                 osub=1024, nslab=2, slabq="sync"):
    extra = None
    if lora is not None:
        not_ = osub // 128
        bf_sb = ar.alloc("bf16", 2 * osub)
        xasl = ar.alloc("bf16", 2 * 512)
        bf_dram = lora["bf_dram"]
        xa_dram = lora["xa_dram"]
        state = {"sub": -1, "tb": -1}

        def extra(og, tb, bi, pb):
            sub = og // not_
            if state["sub"] != sub:
                state["sub"] = sub
                for er in range(2):
                    P.dma("gpsimd", "bf%d" % er,
                          lambda E, S, er=er, sub=sub: E.dma_start(
                              out=bf_sb[:, er * osub:(er + 1) * osub],
                              in_=bf_dram[er * 128:(er + 1) * 128,
                                          sub * osub:(sub + 1) * osub]),
                          reads=[(tag, "bf_dram")], writes=[(tag, "bf", er)])
            if state["tb"] != (sub, tb):
                state["tb"] = (sub, tb)
                t0 = tok0 + tb * 512
                P.dma("sync", "xasl",
                      lambda E, S, t0=t0: E.dma_start(
                          out=xasl,
                          in_=xa_dram.rearrange("(e p) t -> p e t", p=128)[:, :, t0 - tok0:t0 - tok0 + 512]),
                      reads=[(tag, "xa_dram")], writes=[(tag, "xasl")])
            for er in range(2):
                P.op("tensor",
                     lambda E, S, er=er, og=og, pb=pb: E.matmul(
                         pb,
                         bf_sb[:, er * osub + (og % not_) * 128:er * osub + (og % not_) * 128 + 128],
                         xasl[:, er * 512:(er + 1) * 512],
                         start=False, stop=(er == 1)),
                     reads=[(tag, "bf", er), (tag, "xasl")],
                     writes=[bank_key(bi)])

    epi = emit_silu_epilogue(P, ar, tag, g_dram, bias_sb, out_key=gkey)
    return make_gemm_fm_gen(P, ar, tag, w_dram, D, rhs_dram, T, banks, epi,
                            consts, tok0=tok0, w_is_f32=w_is_f32, osub=osub,
                            extra_mms=extra, nslab=nslab, slabq=slabq)


def emit_ffn(P, ar, tag, w_dram, rhs_dram, T, banks, bias_sb, g_dram,
             consts, tok0=0, lora=None, w_is_f32=True, gkey=None, dynregs=None):
    extra = None
    if lora is not None:
        bf_sb = ar.alloc("bf16", 2 * 1024)
        xasl = ar.alloc("bf16", 2 * 512)
        bf_dram = lora["bf_dram"]
        xa_dram = lora["xa_dram"]
        state = {"sub": -1, "tb": -1}

        def extra(og, tb, bi, pb):
            sub = og // 8
            if state["sub"] != sub:
                state["sub"] = sub
                for er in range(2):
                    P.dma("gpsimd", "bf%d" % er,
                          lambda E, S, er=er, sub=sub: E.dma_start(
                              out=bf_sb[:, er * 1024:(er + 1) * 1024],
                              in_=bf_dram[er * 128:(er + 1) * 128,
                                          sub * 1024:(sub + 1) * 1024]),
                          reads=[(tag, "bf_dram")], writes=[(tag, "bf", er)])
            if state["tb"] != (sub, tb):
                state["tb"] = (sub, tb)
                t0 = tok0 + tb * 512
                P.dma("sync", "xasl",
                      lambda E, S, t0=t0: E.dma_start(
                          out=xasl,
                          in_=xa_dram.rearrange("(e p) t -> p e t", p=128)[:, :, t0 - tok0:t0 - tok0 + 512]),
                      reads=[(tag, "xa_dram")], writes=[(tag, "xasl")])
            for er in range(2):
                P.op("tensor",
                     lambda E, S, er=er, og=og, pb=pb: E.matmul(
                         pb[:, :],
                         bf_sb[:, er * 1024 + (og % 8) * 128:er * 1024 + (og % 8) * 128 + 128],
                         xasl[:, er * 512:(er + 1) * 512],
                         start=False, stop=(er == 1)),
                     reads=[(tag, "bf", er), (tag, "xasl")],
                     writes=[bank_key(bi)])

    epi = emit_silu_epilogue(P, ar, tag, g_dram, bias_sb, out_key=gkey)
    emit_gemm_fm(P, ar, tag, w_dram, D, rhs_dram, T, banks, epi, consts,
                 tok0=tok0, w_is_f32=w_is_f32, osub=1024, extra_mms=extra,
                 dynregs=dynregs)



def _rope_tables():
    inv = 10000.0 ** (-np.arange(64, dtype=np.float64) / 64.0)
    pos = np.arange(4096, dtype=np.float64)
    ang = pos[None, :] * inv[:, None]
    cos = np.cos(ang)
    sin = np.sin(ang)
    cos_t = np.concatenate([cos, cos], 0).astype(np.float32)
    sinsign = np.concatenate([-sin, sin], 0).astype(np.float32)
    return cos_t, sinsign


def make_consts_host():
    cos_t, sinsign = _rope_tables()
    return {
        "ones_f": np.ones((128, 1), np.float32),
        "ones_b": np.ones((128, 1), np.float32),
        "onesrow_f": np.ones((1, 128), np.float32),
        "neghalf_f": np.full((1, 128), -0.5, np.float32),
        "negrow_f": np.full((1, 128), -1.0, np.float32),
        "cos_t": cos_t,
        "sinsign": sinsign,
    }


CONST_F32 = ("ones_f", "onesrow_f", "neghalf_f", "negrow_f", "cos_t", "sinsign")
CONST_CAST_BF16 = ("ones_b",)


def declare_consts(nc, P, ar, ch):
    consts = {}
    for n in CONST_F32:
        a = ch[n]
        ext = nc.dram_tensor("c_" + n, list(a.shape), F32, kind="ExternalInput").ap()
        sb = ar.alloc("f32", a.shape[1], parts=a.shape[0])
        consts[n] = sb
        P.dma("sync", "consts",
              lambda E, S, sb=sb, ext=ext: E.dma_start(out=sb, in_=ext[:, :]),
              writes=[("c", n)])
    for n in CONST_CAST_BF16:
        a = ch[n]
        ext = nc.dram_tensor("c_" + n, list(a.shape), F32, kind="ExternalInput").ap()
        sb = ar.alloc("bf16", a.shape[1], parts=a.shape[0])
        consts[n] = sb
        P.dma("gpsimd", "constsb",
              lambda E, S, sb=sb, ext=ext: E.dma_start(out=sb, in_=ext[:, :]),
              writes=[("c", n)])
    return consts


def const_in_map(ch):
    return {"c_" + n: ch[n] for n in CONST_F32 + CONST_CAST_BF16}



def build_ln_test(T):
    nc = bass.Bass("TRN2", target_bir_lowering=False)
    x = nc.dram_tensor("x", [D, T], F32, kind="ExternalInput").ap()
    xn_out = nc.dram_tensor("xn", [D, T], BF16, kind="ExternalOutput").ap()
    ch = make_consts_host()
    P = Prog()
    ctxs = []
    ar = Arena(nc, ctxs)
    P.psum = []
    for i in range(8):
        cm = nc.psum_tensor(f"pb{i}", [128, 512], F32)
        P.psum.append(cm.__enter__())
        ctxs.append(cm)
    consts = declare_consts(nc, P, ar, ch)
    emit_layernorm(P, ar, x, xn_out, T, "ln0", (0, 1, 2, 3), consts,
                   xkey=lambda c, tb: ("xin",),
                   xnkey=lambda c, tb: ("xn", c, tb))
    P.finalize(nc, ["lno0", "lno1"])
    for cm in reversed(ctxs):
        cm.__exit__(None, None, None)
    return nc, ch



def build_program(layers=(0, 1)):
    nc = bass.Bass("TRN2", target_bir_lowering=False)
    ein = lambda n, shp, dt: nc.dram_tensor(n, shp, dt, kind="ExternalInput").ap()

    x_in = ein("x", [D, 4096], F32)
    w0 = {n: ein(n, [D, D], F32) for n in ("w0q", "w0k", "w0v", "w0o", "w0f")}
    a0 = ein("a0", [D, 256], F32)
    b0 = ein("b0", [256, D], F32)
    bias0 = ein("bias0", [128, 32], F32)
    cexp = ein("cexp", [128, 2], F32)
    sexp = ein("sexp", [128, 2], F32)
    w1q = ein("w1q", [D, D], F32)
    w1k = ein("w1k", [D, 1024], F32)
    w1v = ein("w1v", [D, 1024], F32)
    w1o = ein("w1o", [D, D], F32)
    packed = ein("packed", [512, D], I32)
    scale_exp = ein("scale_exp", [512, D], F32)
    bias1 = ein("bias1", [128, 32], F32)

    dram = lambda n, shp, dt: nc.dram_tensor(n, shp, dt).ap()
    xn_d = dram("xn_d", [D, 4096], BF16)
    q_d = dram("q_d", [D, 4096], BF16)
    k_d = dram("k_d", [D, 4096], BF16)
    v_d = dram("v_d", [4096, 4096], BF16)
    at_d = dram("at_d", [D, 4096], BF16)
    g_d = dram("g_d", [D, 4096], F32)
    xa_d = dram("xa_d", [256, 4096], BF16)
    w1_d = dram("w1_d", [D, D], BF16)
    if layers == (0,):
        x1_d = nc.dram_tensor("x1", [D, 4096], F32, kind="ExternalOutput").ap()
    elif layers == (1,):
        x1_d = ein("x1", [D, 4096], F32)
    else:
        x1_d = dram("x1_d", [D, 4096], F32)
    out_e = None
    if 1 in layers:
        out_e = nc.dram_tensor("out", [D, 1024], F32, kind="ExternalOutput").ap()

    ch = make_consts_host()
    ch["negB"] = np.full((128, 1), -SOFTMAX_B, np.float32)

    P = Prog()
    ctxs = []
    ar = Arena(nc, ctxs)
    P.psum = []
    for i in range(4):
        cm = nc.psum_tensor(f"pb{i}", [128, 1024], F32)
        P.psum.append(cm.__enter__())
        ctxs.append(cm)
    consts = declare_consts(nc, P, ar, ch)
    negB_e = ein("c_negB", [128, 1], F32)
    sbB = ar.alloc("f32", 1)
    P.dma("sync", "consts", lambda E, S: E.dma_start(out=sbB, in_=negB_e[:, :]),
          writes=[("c", "negB")])
    consts["negB"] = sbB
    bias_sb = {0: ar.alloc("f32", 32), 1: ar.alloc("f32", 32)}
    P.dma("sync", "consts", lambda E, S: E.dma_start(out=bias_sb[0], in_=bias0[:, :]),
          writes=[("c", "bias")])
    P.dma("sync", "consts", lambda E, S: E.dma_start(out=bias_sb[1], in_=bias1[:, :]),
          writes=[("c", "bias")])
    ce_sb = ar.alloc("f32", 2)
    se_sb = ar.alloc("f32", 2)
    wv_sb = ar.alloc("f32", 2)
    P.dma("sync", "consts", lambda E, S: E.dma_start(out=ce_sb, in_=cexp[:, :]),
          writes=[("c", "ce")])
    P.dma("sync", "consts", lambda E, S: E.dma_start(out=se_sb, in_=sexp[:, :]),
          writes=[("c", "se")])
    P.op("vector", lambda E, S: E.tensor_tensor(out=wv_sb, in0=ce_sb, in1=se_sb,
                                                op=ALU.mult),
         reads=[("c", "ce"), ("c", "se")], writes=[("c", "wvec")])

    vis0, masks0 = build_visits_l0()
    mask_ext0 = {}
    for idx, kb in enumerate(masks0):
        m = masks0[kb]
        ext = ein(f"c_mk{idx}", [128, m.shape[1]], F32)
        ch[f"c_mk{idx}"] = m.astype(np.float32)
        mask_ext0[kb] = ext

    banksA = ((3, 0), (3, 1))

    if 0 in layers:
        emit_layernorm(P, ar, x_in, xn_d, 4096, "ln0",
                       ((0, 0), (0, 1), (1, 0), (1, 1)), consts,
                       xkey=lambda c, tb: ("L0", "xsrc"),
                       xnkey=lambda c, tb: ("L0", "xn", c, tb))
        P.barrier()
        ar.push()
        epi_xa = emit_scale_epilogue(P, ar, "xa0", xa_d, wv_sb)
        emit_gemm_fm(P, ar, "xa0", a0, 256, xn_d, 4096, banksA, epi_xa,
                     consts, osub=256)
        ar.pop()
        P.barrier()
        ar.push()
        epi_q = emit_rope_epilogue(P, ar, "q0", q_d, consts, 0)
        emit_gemm_fm(P, ar, "q0", w0["w0q"], D, xn_d, 4096, banksA, epi_q, consts)
        ar.pop()
        P.barrier()
        ar.push()
        epi_k = emit_rope_epilogue(P, ar, "k0", k_d, consts, 0)
        emit_gemm_fm(P, ar, "k0", w0["w0k"], D, xn_d, 4096, banksA, epi_k, consts)
        ar.pop()
        P.barrier()
        emit_gemm_tm_v(P, ar, "v0", w0["w0v"], D, xn_d, 4096, banksA, v_d,
                       consts, part_major=True, col0=0)
        P.barrier()
        ar.push()
        ffn_gen = make_ffn_gen(P, ar, "f0", w0["w0f"], xn_d, 4096, banksA,
                               bias_sb[0], g_d, consts,
                               lora=dict(bf_dram=b0, xa_dram=xa_d),
                               osub=512, nslab=2, slabq="scalar")
        emit_attention2(P, ar, "a0", q_d, k_d, v_d, at_d, 4096, 4096, 32, 1,
                        consts, mask_ext0, vis0, co=ffn_gen, pulls=0,
                        nkv_buf=1, co_every=10, nprobs=2, nao=1)
        ar.pop()
        P.barrier()
        ar.push()
        epi_r = emit_resid_epilogue(P, ar, "r0", g_d, x_in, x1_d,
                                    gkey=lambda og, tb: ("L0", "g"),
                                    xkey=lambda og, tb: ("L0", "xsrc"),
                                    out_key=lambda og, tb: ("L0", "xo", og, tb),
                                    bias_sb=bias_sb[0])
        emit_gemm_fm(P, ar, "r0", w0["w0o"], D, at_d, 4096, banksA, epi_r, consts)
        ar.pop()
        P.barrier()

    if 1 in layers:
        qinfo = ein("qinfo", [1, 6], I32)
        cosq = ein("cosq", [128, 1024], F32)
        ssq = ein("ssq", [128, 1024], F32)
        k2_d = dram("k2_d", [1024, 5120], BF16)
        v2_d = dram("v2_d", [1024, 5120], BF16)
        xnq_d = dram("xnq_d", [D, 1024], BF16)
        x1q_d = dram("x1q_d", [D, 1024], F32)
        kw_d = dram("kw_d", [1024, 2048], BF16)
        vw_d = dram("vw_d", [1024, 2048], BF16)
        m1_ext = {}
        for st in range(16):
            m1_ext[st] = ein(f"n1_{st}", [128, 1024], F32)
        regs = [nc.sync.alloc_register(name=f"qreg{j}") for j in range(2)]
        dynH = [None, None]

        def initregs(E, S):
            i0_ = E.load(regs[0], qinfo[0:1, 0:1])
            i1_ = E.load(regs[1], qinfo[0:1, 2:3])
            dynH[0] = E.snap(regs[0], min_val=0, max_val=3072)
            dynH[1] = E.snap(regs[1], min_val=0, max_val=3072)
            return i1_
        P.op("sync", initregs, writes=[("qregs",)])

        zext = ein("zeroes", [128, 512], F32)
        zt = ar.alloc("bf16", 512)
        P.dma("gpsimd", "constsb",
              lambda E, S: E.dma_start(out=zt, in_=zext[:, :]),
              writes=[("zt",)])
        for rc in range(8):
            for cc in range(2):
                P.dma("sync", "padz",
                      lambda E, S, rc=rc, cc=cc: E.dma_start(
                          out=k2_d[rc * 128:(rc + 1) * 128, cc * 512:(cc + 1) * 512],
                          in_=zt),
                      reads=[("zt",)], writes=[("kpad", rc, cc)])
                P.dma("sync", "padz",
                      lambda E, S, rc=rc, cc=cc: E.dma_start(
                          out=v2_d[rc * 128:(rc + 1) * 128, cc * 512:(cc + 1) * 512],
                          in_=zt),
                      reads=[("zt",)], writes=[("vpad", rc, cc)])
        P.barrier()
        emit_layernorm(P, ar, x1_d, xn_d, 4096, "ln1",
                       ((0, 0), (0, 1), (1, 0), (1, 1)), consts,
                       xkey=lambda c, tb: ("L1", "xsrc"),
                       xnkey=lambda c, tb: ("L1", "xn", c, tb))
        P.barrier()
        P.dma("sync", "qcp",
              lambda E, S: E.dma_start(out=xnq_d[:, :],
                                       in_=xn_d[:, bass.ds(dynH[0], 1024)]),
              reads=[("qregs",)], writes=[("xnq",)])
        P.dma("sync", "qcp2",
              lambda E, S: E.dma_start(out=x1q_d[:, :],
                                       in_=x1_d[:, bass.ds(dynH[0], 1024)]),
              reads=[("qregs",)], writes=[("x1q",)])
        emit_dequant(P, ar, "dq", packed, scale_exp, w1_d, consts)
        P.barrier()
        ar.push()
        consts_q = dict(consts)
        cq_sb = ar.alloc("f32", 1024)
        sq_sb = ar.alloc("f32", 1024)
        P.dma("sync", "consts", lambda E, S: E.dma_start(out=cq_sb, in_=cosq[:, :]),
              writes=[("c", "cos_t")])
        P.dma("sync", "consts", lambda E, S: E.dma_start(out=sq_sb, in_=ssq[:, :]),
              writes=[("c", "sinsign")])
        consts_q["cos_t"] = cq_sb
        consts_q["sinsign"] = sq_sb
        epi_q = emit_rope_epilogue(P, ar, "q1", q_d, consts_q, 0)
        emit_gemm_fm(P, ar, "q1", w1q, D, xnq_d, 1024, banksA, epi_q, consts)
        ar.pop()
        P.barrier()
        ar.push()
        epi_k = emit_rope_epilogue(P, ar, "k1", k2_d[:, 1024:5120], consts, 0)
        emit_gemm_fm(P, ar, "k1", w1k, 1024, xn_d, 4096, banksA, epi_k, consts)
        ar.pop()
        P.barrier()
        emit_gemm_tm_v(P, ar, "v1", w1v, 1024, xn_d, 4096, banksA,
                       v2_d, consts, part_major=True, col0=1024)
        P.barrier()
        P.dma("sync", "qcp",
              lambda E, S: E.dma_start(out=kw_d[:, :],
                                       in_=k2_d[:, bass.ds(dynH[1], 2048)]),
              reads=[("qregs",)], writes=[("kw",)])
        P.dma("sync", "qcp2",
              lambda E, S: E.dma_start(out=vw_d[:, :],
                                       in_=v2_d[:, bass.ds(dynH[1], 2048)]),
              reads=[("qregs",)], writes=[("vw",)])
        P.barrier()
        ar.push()
        ffn1_gen = make_ffn_gen(P, ar, "f1", w1_d, xnq_d, 1024, banksA,
                                bias_sb[1], g_d, consts, w_is_f32=False,
                                osub=512, nslab=1, slabq="scalar")
        vis1 = {0: [(st, st) for st in range(16)]}
        emit_attention2(P, ar, "a1", q_d, kw_d, vw_d, at_d, 1024, 2048, 32, 4,
                        consts, m1_ext, vis1, co=ffn1_gen, pulls=0,
                        nkv_buf=1, co_every=8)
        ar.pop()
        P.barrier()
        ar.push()
        epi_r = emit_resid_epilogue(P, ar, "r1", g_d, x1q_d, out_e,
                                    gkey=lambda og, tb: ("L1", "g"),
                                    xkey=lambda og, tb: ("x1q",),
                                    out_key=lambda og, tb: ("L1", "xo", og, tb),
                                    bias_sb=bias_sb[1])
        emit_gemm_fm(P, ar, "r1", w1o, D, at_d, 1024, banksA, epi_r, consts)
        ar.pop()

    P.finalize(nc, ["rr0", "rr1"])
    for cm in reversed(ctxs):
        cm.__exit__(None, None, None)
    return nc, ch



def stage_core_inputs(inputs, batch, ch, quarter=0):
    f = np.ascontiguousarray
    x = inputs["L0_attn_x"][batch]
    im = {
        "x": f(x.T).astype(np.float32, copy=False),
        "w0q": f(inputs["L0_attn_Wq"].T),
        "w0k": f(inputs["L0_attn_Wk"].T),
        "w0v": f(inputs["L0_attn_Wv"].T),
        "w0o": f(inputs["L0_attn_Wo"].T),
        "w0f": f(inputs["L0_ffn_W"].T),
        "a0": f(inputs["L0_ffn_A"].reshape(256, D).T),
        "b0": f(inputs["L0_ffn_Bf"].transpose(0, 2, 1).reshape(256, D)),
        "bias0": f(inputs["L0_ffn_bias"].reshape(32, 128).T),
        "cexp": f(inputs["L0_ffn_coeffs"][(np.arange(256) // 16)].reshape(2, 128).T),
        "sexp": f(inputs["L0_ffn_scales"][(np.arange(256) // 16)].reshape(2, 128).T),
        "w1q": f(inputs["L1_attn_wq"]),
        "w1k": f(inputs["L1_attn_wk"]),
        "w1v": f(inputs["L1_attn_wv"]),
        "w1o": f(inputs["L1_attn_wo"]),
        "packed": f(inputs["L1_ffn_packed_weight"].T),
        "scale_exp": f(inputs["L1_ffn_scale"].T[(np.arange(512) // 16)]),
        "bias1": f(inputs["L1_ffn_bias"].reshape(32, 128).T),
    }
    for k in list(ch.keys()):
        if k.startswith("c_") or k in CONST_F32 + CONST_CAST_BF16 or k == "negB":
            pass
    im.update(const_in_map(ch))
    im["c_negB"] = ch["negB"]
    for k, v in ch.items():
        if k.startswith("c_mk"):
            im[k] = v
    im["zeroes"] = np.zeros((128, 512), np.float32)
    tok0 = quarter * 1024
    im["qinfo"] = np.array([[tok0, tok0 + 512, tok0, tok0,
                             tok0 + 1024, 0]], np.int32)
    cos_t, sinsign = ch["cos_t"], ch["sinsign"]
    im["cosq"] = f(cos_t[:, tok0:tok0 + 1024])
    im["ssq"] = f(sinsign[:, tok0:tok0 + 1024])
    for st, m in l1_mask_arrays2(quarter).items():
        im[f"n1_{st}"] = m
    return im


N_LAYER_LAUNCHES = 1


def kernel(**inputs):
    if N_LAYER_LAUNCHES == 1:
        nc, ch = build_program((0, 1))
        in_maps = []
        for c in range(8):
            in_maps.append(stage_core_inputs(inputs, c // 4, ch, c % 4))
        res = run_bass_kernel_spmd(nc, in_maps, core_ids=list(range(8)))
        outs = []
        for b in range(2):
            qs = [res.results[b * 4 + q]["out"].T for q in range(4)]
            outs.append(np.ascontiguousarray(np.concatenate(qs, 0)))
        return np.stack(outs, 0)
    else:
        ncA, chA = build_program((0,))
        ncB, chB = build_program((1,))
        in_mapsA = [stage_core_inputs(inputs, c // 4, chA) for c in range(8)]
        resA = run_bass_kernel_spmd(ncA, in_mapsA, core_ids=list(range(8)))
        in_mapsB = []
        for c in range(8):
            im = stage_core_inputs(inputs, c // 4, chB, c % 4)
            im["x1"] = resA.results[c]["x1"]
            in_mapsB.append(im)
        resB = run_bass_kernel_spmd(ncB, in_mapsB, core_ids=list(range(8)))
        outs = []
        for b in range(2):
            qs = [resB.results[b * 4 + q]["out"].T for q in range(4)]
            outs.append(np.ascontiguousarray(np.concatenate(qs, 0)))
        return np.stack(outs, 0)


# version 12
# speedup vs baseline: 1.8560x; 1.8560x over previous
import numpy as np
import ml_dtypes
import concourse.bass as bass
import concourse.mybir as mybir
from concourse.bass_utils import run_bass_kernel_spmd

F32 = mybir.dt.float32
BF16 = mybir.dt.bfloat16
I32 = mybir.dt.int32
AF = mybir.ActivationFunctionType
ALU = mybir.AluOpType

D = 4096
NCH = 32
HD = 128
EPS = 1e-5
SOFTMAX_B = 30.0
NEG = -1e9

ENGINES = ("tensor", "vector", "scalar", "gpsimd", "sync")

bfdt = ml_dtypes.bfloat16

SAME_ENGINE_SAFE = {"tensor"}



class Prog:

    def __init__(self):
        self.ops = {e: [] for e in ENGINES}
        self.ctr = {e: 0 for e in ENGINES}
        self.sem_names = ["e_" + e for e in ENGINES]
        self.dma_sems = {}
        self.last_write = {}
        self.readers = {}
        self.floors = {e: {} for e in ENGINES}
        self.n_waits = 0
        self.barrier_need = {}

    def dma_sem(self, name):
        name = "d_" + name
        if name not in self.dma_sems:
            self.dma_sems[name] = 0
            self.sem_names.append(name)
        return name

    def _waits_for(self, eng, reads, writes):
        need = {}
        for k in reads:
            t = self.last_write.get(k)
            if t:
                need[t[0]] = max(need.get(t[0], 0), t[1])
        for k in writes:
            t = self.last_write.get(k)
            if t:
                need[t[0]] = max(need.get(t[0], 0), t[1])
            for t in self.readers.get(k, ()):
                need[t[0]] = max(need.get(t[0], 0), t[1])
        for s, v in self.barrier_need.items():
            need[s] = max(need.get(s, 0), v)
        fl = self.floors[eng]
        own = "e_" + eng
        out = []
        for s, v in sorted(need.items()):
            if s == own and eng in SAME_ENGINE_SAFE:
                continue
            if fl.get(s, 0) >= v:
                continue
            out.append((s, v))
            fl[s] = v
        self.n_waits += len(out)
        return out

    def _record(self, tok, reads, writes):
        for k in writes:
            self.last_write[k] = tok
            self.readers[k] = []
        for k in reads:
            self.readers.setdefault(k, []).append(tok)

    def op(self, eng, fn, reads=(), writes=()):
        waits = self._waits_for(eng, reads, writes)
        self.ctr[eng] += 1
        tok = ("e_" + eng, self.ctr[eng])
        self._record(tok, reads, writes)

        def closure(E, sems, waits=tuple(waits), fn=fn, tok=tok):
            for s, v in waits:
                E.wait_ge(sems[s], v)
            inst = fn(E, sems)
            inst.then_inc(sems[tok[0]], 1)
        self.ops[eng].append(closure)
        return tok

    def dma(self, q, slot, fn, reads=(), writes=()):
        sname = self.dma_sem(slot)
        waits = self._waits_for(q, reads, writes)
        pk = ("__slot__", sname)
        t = self.last_write.get(pk)
        if t and self.floors[q].get(t[0], 0) < t[1]:
            waits.append((t[0], t[1]))
            self.floors[q][t[0]] = t[1]
        self.dma_sems[sname] += 16
        tok = (sname, self.dma_sems[sname])
        self._record(tok, reads, writes)
        self.last_write[pk] = tok
        self.readers[pk] = []

        def closure(E, sems, waits=tuple(waits), fn=fn, tok=tok):
            for s, v in waits:
                E.wait_ge(sems[s], v)
            inst = fn(E, sems)
            inst.then_inc(sems[tok[0]], 16)
        self.ops[q].append(closure)
        return tok

    def barrier(self):
        bn = {}
        for e in ENGINES:
            if self.ctr[e]:
                bn["e_" + e] = self.ctr[e]
        for sname, v in self.dma_sems.items():
            if v:
                bn[sname] = v
        self.barrier_need = bn

    def finalize(self, nc, out_dma_slots):
        sems = {}
        ctxs = []
        for n in self.sem_names:
            cm = nc.semaphore(n)
            sems[n] = cm.__enter__()
            ctxs.append(cm)
        blk = nc.Block()
        block = blk.__enter__()
        prog = self

        def make(engname):
            def body(E):
                for cl in prog.ops[engname]:
                    cl(E, sems)
                if engname == "sync":
                    for sl in out_dma_slots:
                        sn = "d_" + sl
                        E.wait_ge(sems[sn], prog.dma_sems[sn])
            return body

        for engname in ENGINES:
            getattr(block, engname)(make(engname))
        blk.__exit__(None, None, None)
        for cm in reversed(ctxs):
            cm.__exit__(None, None, None)


class Arena:

    def __init__(self, nc, ctxs, bf16_cols=70 * 1024, f32_cols=13 * 1024 + 512,
                 i32_cols=2 * 1024 + 64):
        self.tensors = {}
        for name, dt, cols in (("bf16", BF16, bf16_cols), ("f32", F32, f32_cols),
                               ("i32", I32, i32_cols)):
            cm = nc.sbuf_tensor("arena_" + name, [128, cols], dt)
            self.tensors[name] = cm.__enter__()
            ctxs.append(cm)
            setattr(self, "cap_" + name, cols)
        self.off = {"bf16": 0, "f32": 0, "i32": 0}
        self.marks = []

    def alloc(self, kind, cols, parts=128):
        o = self.off[kind]
        assert o + cols <= getattr(self, "cap_" + kind), \
            f"arena {kind} overflow {o}+{cols}"
        self.off[kind] = o + cols
        return self.tensors[kind][0:parts, o:o + cols]

    def push(self):
        self.marks.append(dict(self.off))

    def pop(self):
        self.off = self.marks.pop()


def bank_key(i):
    if isinstance(i, tuple):
        i = i[0]
    return ("bank", i)


def half_bank(P, b):
    i, h = b
    return P.psum[i][:, h * 512:(h + 1) * 512]



def emit_layernorm(P, ar, x_dram, xn_dram, T, tag, psum, consts, xkey, xnkey):
    nt = T // 512
    ar.push()
    xt = [ar.alloc("f32", 512) for _ in range(2)]
    xsq = [ar.alloc("bf16", 512) for _ in range(2)]
    mrow = ar.alloc("f32", 512, parts=1)
    vrow = ar.alloc("f32", 512, parts=1)
    trow = ar.alloc("f32", 512, parts=1)
    rstd = ar.alloc("f32", 512)
    xnt = [ar.alloc("bf16", 512) for _ in range(2)]
    (iS, iQ, iM, iL) = psum
    pS, pQ, pM, pL = (half_bank(P, b) for b in psum)
    ones_f = consts["ones_f"]
    onesrow_f = consts["onesrow_f"]
    neghalf_f = consts["neghalf_f"]
    ones_b = consts["ones_b"]

    for tb in range(nt):
        t0 = tb * 512
        for c in range(NCH):
            buf, kx = xt[c % 2], (tag, "xt", c % 2)
            sq, ksq = xsq[c % 2], (tag, "xsq", c % 2)
            P.dma("sync", "lnx%d" % (c % 2),
                  lambda E, S, buf=buf, c=c, t0=t0: E.dma_start(
                      out=buf, in_=x_dram[c * 128:(c + 1) * 128, t0:t0 + 512]),
                  reads=[xkey(c, tb)], writes=[kx])
            P.op("tensor",
                 lambda E, S, buf=buf, c=c: E.matmul(
                     pS[0:1, :], ones_f, buf, start=(c == 0), stop=(c == NCH - 1)),
                 reads=[kx, ("c", "ones_f")], writes=[bank_key(iS)])
            P.op("scalar",
                 lambda E, S, buf=buf, sq=sq: E.activation(
                     out=sq, in_=buf, func=AF.Square),
                 reads=[kx], writes=[ksq])
            P.op("tensor",
                 lambda E, S, sq=sq, c=c: E.matmul(
                     pQ[0:1, :], ones_b, sq, start=(c == 0), stop=(c == NCH - 1)),
                 reads=[ksq, ("c", "ones_b")], writes=[bank_key(iQ)])
        P.op("vector",
             lambda E, S: E.tensor_scalar(out=mrow, in0=pS[0:1, :],
                                          scalar1=1.0 / D, scalar2=None,
                                          op0=ALU.mult),
             reads=[bank_key(iS)], writes=[(tag, "mrow")])
        P.op("vector",
             lambda E, S: E.tensor_scalar(out=vrow, in0=pQ[0:1, :],
                                          scalar1=1.0 / D, scalar2=EPS,
                                          op0=ALU.mult, op1=ALU.add),
             reads=[bank_key(iQ)], writes=[(tag, "vrow")])
        P.op("vector",
             lambda E, S: E.tensor_tensor(out=trow, in0=mrow, in1=mrow,
                                          op=ALU.mult),
             reads=[(tag, "mrow")], writes=[(tag, "trow")])
        P.op("vector",
             lambda E, S: E.tensor_tensor(out=vrow, in0=vrow, in1=trow,
                                          op=ALU.subtract),
             reads=[(tag, "vrow"), (tag, "trow")], writes=[(tag, "vrow")])
        P.op("scalar",
             lambda E, S: E.activation(out=trow, in_=vrow, func=AF.Ln),
             reads=[(tag, "vrow")], writes=[(tag, "trow")])
        P.op("tensor",
             lambda E, S: E.matmul(pM, onesrow_f, mrow, start=True, stop=True),
             reads=[(tag, "mrow"), ("c", "onesrow_f")], writes=[bank_key(iM)])
        P.op("tensor",
             lambda E, S: E.matmul(pL, neghalf_f, trow, start=True, stop=True),
             reads=[(tag, "trow"), ("c", "neghalf_f")], writes=[bank_key(iL)])
        P.op("scalar",
             lambda E, S: E.activation(out=rstd, in_=pL, func=AF.Exp),
             reads=[bank_key(iL)], writes=[(tag, "rstd")])
        for c in range(NCH):
            buf, kx = xt[c % 2], (tag, "xt", c % 2)
            ot, ko = xnt[c % 2], (tag, "xnt", c % 2)
            P.dma("sync", "lnx%d" % (c % 2),
                  lambda E, S, buf=buf, c=c, t0=t0: E.dma_start(
                      out=buf, in_=x_dram[c * 128:(c + 1) * 128, t0:t0 + 512]),
                  reads=[xkey(c, tb)], writes=[kx])
            P.op("vector",
                 lambda E, S, buf=buf: E.tensor_tensor(
                     out=buf, in0=buf, in1=pM, op=ALU.subtract),
                 reads=[kx, bank_key(iM)], writes=[kx])
            P.op("vector",
                 lambda E, S, buf=buf, ot=ot: E.tensor_tensor(
                     out=ot, in0=buf, in1=rstd, op=ALU.mult),
                 reads=[kx, (tag, "rstd")], writes=[ko])
            P.dma("sync", "lno%d" % (c % 2),
                  lambda E, S, ot=ot, c=c, t0=t0: E.dma_start(
                      out=xn_dram[c * 128:(c + 1) * 128, t0:t0 + 512], in_=ot),
                  reads=[ko], writes=[xnkey(c, tb)])
    ar.pop()


def make_gemm_fm_gen(P, ar, tag, w_dram, Dout, rhs_dram, T, banks, epilogue,
                     consts, tok0=0, w_is_f32=True, osub=1024, extra_mms=None,
                     rhs_key=None, w_key=None, nslab=2, slabq="sync"):
    nt = T // 512
    nsub = (Dout + osub - 1) // osub
    wsb = ar.alloc("bf16", NCH * osub)
    slab = [ar.alloc("bf16", NCH * 512) for _ in range(nslab)]
    if rhs_key is None:
        rhs_key = lambda: (tag, "rhs_dram")
    if w_key is None:
        w_key = lambda: (tag, "w_dram")
    rr = rhs_dram.rearrange("(c p) t -> p c t", p=128)

    def gen():
        for sub in range(nsub):
            o0 = sub * osub
            ow = min(osub, Dout - o0)
            for c in range(NCH):
                q = "gpsimd" if w_is_f32 else "sync"
                P.dma(q, "gw%d" % (c % 2),
                      lambda E, S, c=c, o0=o0, ow=ow: E.dma_start(
                          out=wsb[:, c * osub:c * osub + ow],
                          in_=w_dram[c * 128:(c + 1) * 128, o0:o0 + ow]),
                      reads=[w_key()], writes=[(tag, "w", c)])
            for tb in range(nt):
                sl = slab[tb % nslab]
                ksl = (tag, "slab", tb % nslab)
                t0 = tok0 + tb * 512
                for piece in range(4):
                    P.dma(slabq, "gs%d_%d" % (tb % nslab, piece),
                          lambda E, S, sl=sl, piece=piece, t0=t0: E.dma_start(
                              out=sl[:, piece * 8 * 512:(piece + 1) * 8 * 512],
                              in_=rr[:, piece * 8:(piece + 1) * 8, t0:t0 + 512]),
                          reads=[rhs_key()], writes=[(ksl, piece)])
                slab_keys = [(ksl, p) for p in range(4)]
                for ot in range(ow // 128):
                    og = (o0 + ot * 128) // 128
                    bi = banks[(sub * nt * (osub // 128) + tb * (ow // 128) + ot) % 2]
                    pb = half_bank(P, bi)
                    for c in range(NCH):
                        last = (c == NCH - 1) and extra_mms is None
                        P.op("tensor",
                             lambda E, S, c=c, ot=ot, pb=pb, last=last, sl=sl: E.matmul(
                                 pb, wsb[:, c * osub + ot * 128:c * osub + ot * 128 + 128],
                                 sl[:, c * 512:(c + 1) * 512],
                                 start=(c == 0), stop=last),
                             reads=[(tag, "w", c), slab_keys[c // 8]],
                             writes=[bank_key(bi)])
                    if extra_mms is not None:
                        extra_mms(og, tb, bi, pb)
                    epilogue(og, tb, bi, pb)
                    yield
    return gen()


def emit_gemm_fm(P, ar, tag, w_dram, Dout, rhs_dram, T, banks, epilogue,
                 consts, tok0=0, w_is_f32=True, osub=1024, extra_mms=None,
                 rhs_key=None, w_key=None):
    ar.push()
    g = make_gemm_fm_gen(P, ar, tag, w_dram, Dout, rhs_dram, T, banks,
                         epilogue, consts, tok0, w_is_f32, osub, extra_mms,
                         rhs_key, w_key)
    for _ in g:
        pass
    ar.pop()


def emit_rope_epilogue(P, ar, tag, out_dram, consts, tok0, out_key=None):
    ev = [ar.alloc("f32", 512) for _ in range(2)]
    t1 = [ar.alloc("f32", 512) for _ in range(2)]
    t2 = [ar.alloc("f32", 512) for _ in range(2)]
    ro = [ar.alloc("bf16", 512) for _ in range(2)]
    cos_t = consts["cos_t"]
    ss = consts["sinsign"]
    ctr = [0]

    def epi(og, tb, bi, pb):
        i = ctr[0] % 2
        ctr[0] += 1
        t0 = tok0 + tb * 512
        kev, kt1, kt2, kro = (tag, "ev", i), (tag, "t1", i), (tag, "t2", i), (tag, "ro", i)
        P.op("scalar",
             lambda E, S, i=i, pb=pb: E.activation(out=ev[i], in_=pb, func=AF.Copy),
             reads=[bank_key(bi)], writes=[kev])
        P.op("vector",
             lambda E, S, i=i, t0=t0: E.tensor_tensor(
                 out=t1[i], in0=ev[i], in1=cos_t[:, t0:t0 + 512], op=ALU.mult),
             reads=[kev, ("c", "cos_t")], writes=[kt1])
        P.op("vector",
             lambda E, S, i=i, pb=pb, t0=t0: E.tensor_tensor(
                 out=t2[i][0:64, :], in0=pb[64:128, :],
                 in1=ss[0:64, t0:t0 + 512], op=ALU.mult),
             reads=[bank_key(bi), ("c", "sinsign")], writes=[(kt2, "lo")])
        P.op("vector",
             lambda E, S, i=i, pb=pb, t0=t0: E.tensor_tensor(
                 out=t2[i][64:128, :], in0=pb[0:64, :],
                 in1=ss[64:128, t0:t0 + 512], op=ALU.mult),
             reads=[bank_key(bi), ("c", "sinsign")], writes=[(kt2, "hi")])
        P.op("vector",
             lambda E, S, i=i: E.tensor_tensor(
                 out=ro[i], in0=t1[i], in1=t2[i], op=ALU.add),
             reads=[kt1, (kt2, "lo"), (kt2, "hi")], writes=[kro])
        P.dma("sync", "go%d" % i,
              lambda E, S, i=i, og=og, tb=tb: E.dma_start(
                  out=out_dram[og * 128:(og + 1) * 128, tb * 512:(tb + 1) * 512],
                  in_=ro[i]),
              reads=[kro],
              writes=[out_key(og, tb) if out_key else (tag, "out", og, tb)])
    return epi


def emit_copy_bf16_epilogue(P, ar, tag, out_dram, out_key=None):
    ob = [ar.alloc("bf16", 512) for _ in range(2)]
    ctr = [0]

    def epi(og, tb, bi, pb):
        i = ctr[0] % 2
        ctr[0] += 1
        ko = (tag, "ob", i)
        P.op("scalar",
             lambda E, S, i=i, pb=pb: E.activation(out=ob[i], in_=pb, func=AF.Copy),
             reads=[bank_key(bi)], writes=[ko])
        P.dma("sync", "go%d" % i,
              lambda E, S, i=i, og=og, tb=tb: E.dma_start(
                  out=out_dram[og * 128:(og + 1) * 128, tb * 512:(tb + 1) * 512],
                  in_=ob[i]),
              reads=[ko],
              writes=[out_key(og, tb) if out_key else (tag, "out", og, tb)])
    return epi


def emit_gemm_tm_v(P, ar, tag, w_dram, Dout, rhs_dram, T, banks, v_dram,
                   consts, tok0=0, rhs_key=None, out_key=None,
                   part_major=False, col0=0):
    nt = T // 512
    nsub = Dout // 512
    ar.push()
    wsb = ar.alloc("bf16", NCH * 512)
    slab = [ar.alloc("bf16", NCH * 512) for _ in range(2)]
    vo = [ar.alloc("bf16", 512) for _ in range(2)]
    if rhs_key is None:
        rhs_key = lambda: (tag, "rhs_dram")
    rr = rhs_dram.rearrange("(c p) t -> p c t", p=128)
    vv = None if part_major else v_dram.rearrange("h t d -> t h d")
    ctr = [0]
    for sub in range(nsub):
        o0 = sub * 512
        for c in range(NCH):
            P.dma("gpsimd", "gw%d" % (c % 2),
                  lambda E, S, c=c, o0=o0: E.dma_start(
                      out=wsb[:, c * 512:(c + 1) * 512],
                      in_=w_dram[c * 128:(c + 1) * 128, o0:o0 + 512]),
                  reads=[(tag, "w_dram")], writes=[(tag, "w", c)])
        for tb in range(nt):
            sl = slab[tb % 2]
            ksl = (tag, "slab", tb % 2)
            t0 = tok0 + tb * 512
            for piece in range(4):
                P.dma("sync", "gs%d_%d" % (tb % 2, piece),
                      lambda E, S, sl=sl, piece=piece, t0=t0: E.dma_start(
                          out=sl[:, piece * 8 * 512:(piece + 1) * 8 * 512],
                          in_=rr[:, piece * 8:(piece + 1) * 8, t0:t0 + 512]),
                      reads=[rhs_key()], writes=[(ksl, piece)])
            slab_keys = [(ksl, p) for p in range(4)]
            for tt in range(4):
                bi = banks[ctr[0] % 2]
                i = ctr[0] % 2
                ctr[0] += 1
                pb = half_bank(P, bi)
                for c in range(NCH):
                    P.op("tensor",
                         lambda E, S, c=c, tt=tt, pb=pb, sl=sl: E.matmul(
                             pb,
                             sl[:, c * 512 + tt * 128:c * 512 + tt * 128 + 128],
                             wsb[:, c * 512:(c + 1) * 512],
                             start=(c == 0), stop=(c == NCH - 1)),
                         reads=[(tag, "w", c), slab_keys[c // 8]],
                         writes=[bank_key(bi)])
                ko = (tag, "vo", i)
                P.op("scalar",
                     lambda E, S, i=i, pb=pb: E.activation(
                         out=vo[i], in_=pb, func=AF.Copy),
                     reads=[bank_key(bi)], writes=[ko])
                tg = tb * 512 + tt * 128
                if not part_major:
                    P.dma("sync", "go%d" % i,
                          lambda E, S, i=i, tg=tg, sub=sub: E.dma_start(
                              out=vv[tg:tg + 128, sub * 4:(sub + 1) * 4, :], in_=vo[i]),
                          reads=[ko],
                          writes=[out_key(sub, tg) if out_key else (tag, "vout", sub, tg)])
                else:
                    for hl in range(4):
                        kv = sub * 4 + hl
                        P.dma("sync", "go%d" % i,
                              lambda E, S, i=i, tg=tg, kv=kv, hl=hl: E.dma_start(
                                  out=v_dram[kv * 128:(kv + 1) * 128,
                                             col0 + tg:col0 + tg + 128],
                                  in_=vo[i][:, hl * 128:(hl + 1) * 128]),
                              reads=[ko],
                              writes=[(tag, "vout", kv, tg)])
    ar.pop()


def emit_attention2(P, ar, tag, q_dram, k_dram, v_dram, o_dram, Tq, Tk,
                    nheads, grp, consts, mask_ext, visits, co=None, pulls=0,
                    nkv_buf=2, co_every=None, mask_w=1024, nprobs=3, nao=2):
    nts = Tq // 1024
    ar.push()
    masksb = {}
    for mid, ext in mask_ext.items():
        msb = ar.alloc("bf16", ext.shape[-1] if hasattr(ext, "shape") else mask_w)
        masksb[mid] = msb
        P.dma("gpsimd", "constsb",
              lambda E, S, msb=msb, ext=ext: E.dma_start(out=msb, in_=ext[:, :]),
              writes=[("c", "mask", mid)])
    qsl = [ar.alloc("bf16", 1024) for _ in range(2)]
    ksl = [ar.alloc("bf16", Tk) for _ in range(nkv_buf)]
    vsl = [ar.alloc("bf16", Tk) for _ in range(nkv_buf)]
    probs = [ar.alloc("bf16", 1024) for _ in range(nprobs)]
    acc = [ar.alloc("f32", 1024) for _ in range(2)]
    lnrow = ar.alloc("f32", 1024, parts=1)
    rbc = ar.alloc("f32", 1024)
    ao = [ar.alloc("bf16", 1024) for _ in range(nao)]
    pdS = (0, 1)
    iPA = 2
    pa = P.psum[iPA]
    ones_f = consts["ones_f"]
    negrow_f = consts["negrow_f"]
    negB = consts["negB"]
    sm = HD ** -0.5
    pctr = [0]
    scctr = [0]
    qctr = [0]
    aoctr = [0]

    vctr = [0]
    for h in range(nheads):
        kvh = h // grp
        kvb = kvh % nkv_buf
        kk = (tag, "ksl", kvb)
        kv = (tag, "vsl", kvb)
        if h % grp == 0:
            P.dma("sync", "ak%d" % kvb,
                  lambda E, S, kvh=kvh, kvb=kvb: E.dma_start(
                      out=ksl[kvb], in_=k_dram[kvh * 128:(kvh + 1) * 128, 0:Tk]),
                  reads=[(tag, "k_dram")], writes=[kk])
            P.dma("sync", "av%d" % kvb,
                  lambda E, S, kvh=kvh, kvb=kvb: E.dma_start(
                      out=vsl[kvb], in_=v_dram[kvh * 128:(kvh + 1) * 128, 0:Tk]),
                  reads=[(tag, "v_dram")], writes=[kv])
        for ts in range(nts):
            qb = qctr[0] % 2
            qctr[0] += 1
            kq = (tag, "qsl", qb)
            P.dma("sync", "aq%d" % qb,
                  lambda E, S, h=h, ts=ts, qb=qb: E.dma_start(
                      out=qsl[qb], in_=q_dram[h * 128:(h + 1) * 128,
                                              ts * 1024:(ts + 1) * 1024]),
                  reads=[(tag, "q_dram")], writes=[kq])
            sts = visits[ts]
            nst = len(sts)
            par = (h * nts + ts) % 2
            ka = (tag, "acc", par)

            def emit_sc(idx):
                st, mid = sts[idx]
                si = pdS[scctr[0] % 2]
                scctr[0] += 1
                psc = P.psum[si]
                for hf in range(2):
                    P.op("tensor",
                         lambda E, S, st=st, psc=psc, kvb=kvb, qb=qb, hf=hf: E.matmul(
                             psc[:, hf * 512:(hf + 1) * 512],
                             ksl[kvb][:, st * 128:(st + 1) * 128],
                             qsl[qb][:, hf * 512:(hf + 1) * 512],
                             start=True, stop=True),
                         reads=[kk, kq], writes=[bank_key(si)])
                return si

            def emit_probs(idx, si):
                st, mid = sts[idx]
                psc = P.psum[si]
                pi = pctr[0] % nprobs
                pctr[0] += 1
                kp = (tag, "probs", pi)
                P.op("scalar",
                     lambda E, S, pi=pi, psc=psc: E.activation(
                         out=probs[pi], in_=psc[:, :], func=AF.Exp,
                         scale=sm, bias=negB),
                     reads=[bank_key(si), ("c", "negB")], writes=[kp])
                if mid is not None:
                    if isinstance(mid, tuple) and mid[0] == "__half__":
                        _, mkey, mhalf = mid
                        P.op("vector",
                             lambda E, S, pi=pi, mkey=mkey, mhalf=mhalf: E.tensor_tensor(
                                 out=probs[pi][:, mhalf * 512:(mhalf + 1) * 512],
                                 in0=probs[pi][:, mhalf * 512:(mhalf + 1) * 512],
                                 in1=masksb[mkey], op=ALU.mult),
                             reads=[kp, ("c", "mask", mkey)], writes=[kp])
                    else:
                        P.op("vector",
                             lambda E, S, pi=pi, mid=mid: E.tensor_tensor(
                                 out=probs[pi], in0=probs[pi], in1=masksb[mid],
                                 op=ALU.mult),
                             reads=[kp, ("c", "mask", mid)], writes=[kp])
                return pi

            def emit_cons(idx, pi):
                st, mid = sts[idx]
                kp = (tag, "probs", pi)
                if idx == 0:
                    P.op("vector",
                         lambda E, S, pi=pi, par=par: E.tensor_copy(
                             out=acc[par], in_=probs[pi]),
                         reads=[kp], writes=[ka])
                else:
                    P.op("vector",
                         lambda E, S, pi=pi, par=par: E.tensor_tensor(
                             out=acc[par], in0=acc[par], in1=probs[pi], op=ALU.add),
                         reads=[kp, ka], writes=[ka])
                for hf in range(2):
                    P.op("tensor",
                         lambda E, S, st=st, pi=pi, idx=idx, nst=nst, kvb=kvb, hf=hf: E.matmul(
                             pa[:, hf * 512:(hf + 1) * 512],
                             vsl[kvb][:, st * 128:(st + 1) * 128],
                             probs[pi][:, hf * 512:(hf + 1) * 512],
                             start=(idx == 0), stop=(idx == nst - 1)),
                         reads=[kv, kp],
                         writes=[bank_key(iPA)])
                vctr[0] += 1
                if co is not None and co_every and vctr[0] % co_every == 0:
                    next(co, None)

            si_prev = emit_sc(0)
            pi_prev = emit_probs(0, si_prev)
            for idx in range(1, nst):
                si = emit_sc(idx)
                emit_cons(idx - 1, pi_prev)
                pi_prev = emit_probs(idx, si)
            emit_cons(nst - 1, pi_prev)
            for hf in range(2):
                P.op("tensor",
                     lambda E, S, par=par, hf=hf: E.matmul(
                         P.psum[0][0:1, hf * 512:(hf + 1) * 512], ones_f,
                         acc[par][:, hf * 512:(hf + 1) * 512],
                         start=True, stop=True),
                     reads=[ka, ("c", "ones_f")],
                     writes=[bank_key(0)])
            P.op("scalar",
                 lambda E, S: E.activation(out=lnrow, in_=P.psum[0][0:1, :],
                                           func=AF.Ln),
                 reads=[bank_key(0)], writes=[(tag, "lnrow")])
            for hf in range(2):
                P.op("tensor",
                     lambda E, S, hf=hf: E.matmul(
                         P.psum[1][:, hf * 512:(hf + 1) * 512], negrow_f,
                         lnrow[0:1, hf * 512:(hf + 1) * 512],
                         start=True, stop=True),
                     reads=[(tag, "lnrow"), ("c", "negrow_f")],
                     writes=[bank_key(1)])
            P.op("scalar",
                 lambda E, S: E.activation(out=rbc, in_=P.psum[1][:, :], func=AF.Exp),
                 reads=[bank_key(1)], writes=[(tag, "rbc")])
            ai = aoctr[0] % nao
            aoctr[0] += 1
            kao = (tag, "ao", ai)
            P.op("vector",
                 lambda E, S, ai=ai: E.tensor_tensor(
                     out=ao[ai], in0=pa[:, :], in1=rbc, op=ALU.mult),
                 reads=[bank_key(iPA), (tag, "rbc")], writes=[kao])
            P.dma("sync", "aao%d" % ai,
                  lambda E, S, h=h, ts=ts, ai=ai: E.dma_start(
                      out=o_dram[h * 128:(h + 1) * 128, ts * 1024:(ts + 1) * 1024],
                      in_=ao[ai]),
                  reads=[kao], writes=[(tag, "attnT", h, ts)])
            if co is not None:
                for _ in range(pulls):
                    next(co, None)
    if co is not None:
        for _ in co:
            pass
    ar.pop()


def build_visits_l0(tsup=1024):
    visits = {}
    masks = {}
    for ts in range(4096 // tsup):
        lst = []
        for st in range(4096 // 128):
            gs = st * 128 + np.arange(128)[:, None]
            gt = ts * tsup + np.arange(tsup)[None, :]
            m = gs <= gt
            if not m.any():
                continue
            if m.all():
                lst.append((st, None))
            else:
                half = 0 if (st * 128) < (ts * tsup + 512) else 1
                if half == 0:
                    mh = m[:, 0:512]
                    kb = mh.tobytes()
                    masks.setdefault(kb, mh)
                    lst.append((st, ("__half__", kb, 0)))
                else:
                    kb = m.tobytes()
                    masks.setdefault(kb, m)
                    lst.append((st, kb))
        visits[ts] = lst
    return visits, masks


def l1_mask_arrays2(quarter):
    tok0 = quarter * 1024
    out = {}
    for st in range(16):
        gs = tok0 - 1024 + st * 128 + np.arange(128)[:, None]
        gt = tok0 + np.arange(1024)[None, :]
        allowed = (gs >= 0) & (gs <= gt) & (gs > gt - 1024)
        out[st] = allowed.astype(np.float32)
    return out


def emit_silu_epilogue(P, ar, tag, g_dram, bias_sb, out_key=None):
    gt = [ar.alloc("f32", 512) for _ in range(2)]
    ctr = [0]

    def epi(og, tb, bi, pb):
        i = ctr[0] % 2
        ctr[0] += 1
        ko = (tag, "gt", i)
        P.op("vector",
             lambda E, S, i=i, pb=pb: E.tensor_copy(out=gt[i], in_=pb),
             reads=[bank_key(bi)], writes=[ko])
        P.dma("sync", "go%d" % i,
              lambda E, S, i=i, og=og, tb=tb: E.dma_start(
                  out=g_dram[og * 128:(og + 1) * 128, tb * 512:(tb + 1) * 512],
                  in_=gt[i]),
              reads=[ko],
              writes=[out_key(og, tb) if out_key else (tag, "g", og, tb)])
    return epi


def emit_scale_epilogue(P, ar, tag, out_dram, wvec_sb, out_key=None):
    ot_ = [ar.alloc("bf16", 512) for _ in range(2)]
    ctr = [0]

    def epi(og, tb, bi, pb):
        i = ctr[0] % 2
        ctr[0] += 1
        ko = (tag, "xa", i)
        P.op("scalar",
             lambda E, S, i=i, pb=pb, og=og: E.activation(
                 out=ot_[i], in_=pb, func=AF.Copy,
                 scale=wvec_sb[:, og:og + 1]),
             reads=[bank_key(bi), ("c", "wvec")], writes=[ko])
        P.dma("sync", "go%d" % i,
              lambda E, S, i=i, og=og, tb=tb: E.dma_start(
                  out=out_dram[og * 128:(og + 1) * 128, tb * 512:(tb + 1) * 512],
                  in_=ot_[i]),
              reads=[ko],
              writes=[out_key(og, tb) if out_key else (tag, "out", og, tb)])
    return epi


def emit_resid_epilogue(P, ar, tag, g_dram, x_src, x_dst, gkey, xkey, out_key,
                        bias_sb, tok0=0):
    gtile = [ar.alloc("f32", 512) for _ in range(2)]
    xtile = [ar.alloc("f32", 512) for _ in range(2)]
    ut = [ar.alloc("f32", 512) for _ in range(2)]
    ctr = [0]

    def epi(og, tb, bi, pb):
        i = ctr[0] % 2
        ctr[0] += 1
        t0 = tok0 + tb * 512
        kg, kx, ku = (tag, "gtile", i), (tag, "xtile", i), (tag, "ut", i)
        P.dma("sync", "rg%d" % i,
              lambda E, S, i=i, og=og, t0=t0: E.dma_start(
                  out=gtile[i], in_=g_dram[og * 128:(og + 1) * 128, t0:t0 + 512]),
              reads=[gkey(og, tb)], writes=[kg])
        P.dma("sync", "rx%d" % i,
              lambda E, S, i=i, og=og, t0=t0: E.dma_start(
                  out=xtile[i], in_=x_src[og * 128:(og + 1) * 128, t0:t0 + 512]),
              reads=[xkey(og, tb)], writes=[kx])
        P.op("scalar",
             lambda E, S, i=i, og=og: E.activation(
                 out=gtile[i], in_=gtile[i], func=AF.Silu,
                 bias=bias_sb[:, og:og + 1]),
             reads=[kg, ("c", "bias")], writes=[kg])
        P.op("vector",
             lambda E, S, i=i, pb=pb: E.tensor_tensor(
                 out=ut[i], in0=gtile[i], in1=pb, op=ALU.mult),
             reads=[kg, bank_key(bi)], writes=[ku])
        P.op("vector",
             lambda E, S, i=i: E.tensor_tensor(
                 out=ut[i], in0=ut[i], in1=xtile[i], op=ALU.add),
             reads=[ku, kx], writes=[ku])
        P.dma("sync", "rr%d" % i,
              lambda E, S, i=i, og=og, t0=t0: E.dma_start(
                  out=x_dst[og * 128:(og + 1) * 128, t0:t0 + 512], in_=ut[i]),
              reads=[ku], writes=[out_key(og, tb)])
    return epi


def emit_dequant(P, ar, tag, packed_dram, scale_dram, w_out, consts):
    ar.push()
    psb = [ar.alloc("i32", 1024)]
    nib = ar.alloc("i32", 1024)
    ssb = [ar.alloc("f32", 1024)]
    ob = [ar.alloc("bf16", 1024) for _ in range(2)]
    for pc in range(4):
        for half in range(4):
            i = 0
            o0 = half * 1024
            kp, ks = (tag, "psb", i), (tag, "ssb", i)
            P.dma("sync", "dqp",
                  lambda E, S, pc=pc, o0=o0, i=i: E.dma_start(
                      out=psb[i], in_=packed_dram[pc * 128:(pc + 1) * 128,
                                                  o0:o0 + 1024]),
                  reads=[(tag, "packed")], writes=[kp])
            P.dma("sync", "dqs",
                  lambda E, S, pc=pc, o0=o0, i=i: E.dma_start(
                      out=ssb[i], in_=scale_dram[pc * 128:(pc + 1) * 128,
                                                 o0:o0 + 1024]),
                  reads=[(tag, "scales")], writes=[ks])
            for e in range(8):
                kn = (tag, "nib")
                ko = (tag, "ob", e % 2)
                P.op("vector",
                     lambda E, S, e=e, i=i: E.tensor_scalar(
                         out=nib, in0=psb[i], scalar1=4 * e, scalar2=28,
                         op0=ALU.logical_shift_left, op1=ALU.arith_shift_right),
                     reads=[kp], writes=[kn])
                P.op("vector",
                     lambda E, S, e=e, i=i: E.tensor_tensor(
                         out=ob[e % 2], in0=nib, in1=ssb[i], op=ALU.mult),
                     reads=[kn, ks], writes=[ko])
                wo = w_out.rearrange("(r eight) o -> r eight o", eight=8)
                P.dma("scalar", "dqo%d" % (e % 2),
                      lambda E, S, e=e, pc=pc, o0=o0, wo=wo: E.dma_start(
                          out=wo[pc * 128:(pc + 1) * 128, e, o0:o0 + 1024],
                          in_=ob[e % 2]),
                      reads=[ko], writes=[(tag, "wout", pc, e, o0)])
    ar.pop()


def make_ffn_gen(P, ar, tag, w_dram, rhs_dram, T, banks, bias_sb, g_dram,
                 consts, tok0=0, lora=None, w_is_f32=True, gkey=None,
                 osub=1024, nslab=2, slabq="sync"):
    extra = None
    if lora is not None:
        not_ = osub // 128
        bf_sb = ar.alloc("bf16", 2 * osub)
        xasl = ar.alloc("bf16", 2 * 512)
        bf_dram = lora["bf_dram"]
        xa_dram = lora["xa_dram"]
        state = {"sub": -1, "tb": -1}

        def extra(og, tb, bi, pb):
            sub = og // not_
            if state["sub"] != sub:
                state["sub"] = sub
                for er in range(2):
                    P.dma("gpsimd", "bf%d" % er,
                          lambda E, S, er=er, sub=sub: E.dma_start(
                              out=bf_sb[:, er * osub:(er + 1) * osub],
                              in_=bf_dram[er * 128:(er + 1) * 128,
                                          sub * osub:(sub + 1) * osub]),
                          reads=[(tag, "bf_dram")], writes=[(tag, "bf", er)])
            if state["tb"] != (sub, tb):
                state["tb"] = (sub, tb)
                t0 = tok0 + tb * 512
                P.dma("sync", "xasl",
                      lambda E, S, t0=t0: E.dma_start(
                          out=xasl,
                          in_=xa_dram.rearrange("(e p) t -> p e t", p=128)[:, :, t0 - tok0:t0 - tok0 + 512]),
                      reads=[(tag, "xa_dram")], writes=[(tag, "xasl")])
            for er in range(2):
                P.op("tensor",
                     lambda E, S, er=er, og=og, pb=pb: E.matmul(
                         pb,
                         bf_sb[:, er * osub + (og % not_) * 128:er * osub + (og % not_) * 128 + 128],
                         xasl[:, er * 512:(er + 1) * 512],
                         start=False, stop=(er == 1)),
                     reads=[(tag, "bf", er), (tag, "xasl")],
                     writes=[bank_key(bi)])

    epi = emit_silu_epilogue(P, ar, tag, g_dram, bias_sb, out_key=gkey)
    return make_gemm_fm_gen(P, ar, tag, w_dram, D, rhs_dram, T, banks, epi,
                            consts, tok0=tok0, w_is_f32=w_is_f32, osub=osub,
                            extra_mms=extra, nslab=nslab, slabq=slabq)


def emit_ffn(P, ar, tag, w_dram, rhs_dram, T, banks, bias_sb, g_dram,
             consts, tok0=0, lora=None, w_is_f32=True, gkey=None, dynregs=None):
    extra = None
    if lora is not None:
        bf_sb = ar.alloc("bf16", 2 * 1024)
        xasl = ar.alloc("bf16", 2 * 512)
        bf_dram = lora["bf_dram"]
        xa_dram = lora["xa_dram"]
        state = {"sub": -1, "tb": -1}

        def extra(og, tb, bi, pb):
            sub = og // 8
            if state["sub"] != sub:
                state["sub"] = sub
                for er in range(2):
                    P.dma("gpsimd", "bf%d" % er,
                          lambda E, S, er=er, sub=sub: E.dma_start(
                              out=bf_sb[:, er * 1024:(er + 1) * 1024],
                              in_=bf_dram[er * 128:(er + 1) * 128,
                                          sub * 1024:(sub + 1) * 1024]),
                          reads=[(tag, "bf_dram")], writes=[(tag, "bf", er)])
            if state["tb"] != (sub, tb):
                state["tb"] = (sub, tb)
                t0 = tok0 + tb * 512
                P.dma("sync", "xasl",
                      lambda E, S, t0=t0: E.dma_start(
                          out=xasl,
                          in_=xa_dram.rearrange("(e p) t -> p e t", p=128)[:, :, t0 - tok0:t0 - tok0 + 512]),
                      reads=[(tag, "xa_dram")], writes=[(tag, "xasl")])
            for er in range(2):
                P.op("tensor",
                     lambda E, S, er=er, og=og, pb=pb: E.matmul(
                         pb[:, :],
                         bf_sb[:, er * 1024 + (og % 8) * 128:er * 1024 + (og % 8) * 128 + 128],
                         xasl[:, er * 512:(er + 1) * 512],
                         start=False, stop=(er == 1)),
                     reads=[(tag, "bf", er), (tag, "xasl")],
                     writes=[bank_key(bi)])

    epi = emit_silu_epilogue(P, ar, tag, g_dram, bias_sb, out_key=gkey)
    emit_gemm_fm(P, ar, tag, w_dram, D, rhs_dram, T, banks, epi, consts,
                 tok0=tok0, w_is_f32=w_is_f32, osub=1024, extra_mms=extra,
                 dynregs=dynregs)



def _rope_tables():
    inv = 10000.0 ** (-np.arange(64, dtype=np.float64) / 64.0)
    pos = np.arange(4096, dtype=np.float64)
    ang = pos[None, :] * inv[:, None]
    cos = np.cos(ang)
    sin = np.sin(ang)
    cos_t = np.concatenate([cos, cos], 0).astype(np.float32)
    sinsign = np.concatenate([-sin, sin], 0).astype(np.float32)
    return cos_t, sinsign


def make_consts_host():
    cos_t, sinsign = _rope_tables()
    return {
        "ones_f": np.ones((128, 1), np.float32),
        "ones_b": np.ones((128, 1), np.float32),
        "onesrow_f": np.ones((1, 128), np.float32),
        "neghalf_f": np.full((1, 128), -0.5, np.float32),
        "negrow_f": np.full((1, 128), -1.0, np.float32),
        "cos_t": cos_t,
        "sinsign": sinsign,
    }


CONST_F32 = ("ones_f", "onesrow_f", "neghalf_f", "negrow_f", "cos_t", "sinsign")
CONST_CAST_BF16 = ("ones_b",)


def declare_consts(nc, P, ar, ch):
    consts = {}
    for n in CONST_F32:
        a = ch[n]
        ext = nc.dram_tensor("c_" + n, list(a.shape), F32, kind="ExternalInput").ap()
        sb = ar.alloc("f32", a.shape[1], parts=a.shape[0])
        consts[n] = sb
        P.dma("sync", "consts",
              lambda E, S, sb=sb, ext=ext: E.dma_start(out=sb, in_=ext[:, :]),
              writes=[("c", n)])
    for n in CONST_CAST_BF16:
        a = ch[n]
        ext = nc.dram_tensor("c_" + n, list(a.shape), F32, kind="ExternalInput").ap()
        sb = ar.alloc("bf16", a.shape[1], parts=a.shape[0])
        consts[n] = sb
        P.dma("gpsimd", "constsb",
              lambda E, S, sb=sb, ext=ext: E.dma_start(out=sb, in_=ext[:, :]),
              writes=[("c", n)])
    return consts


def const_in_map(ch):
    return {"c_" + n: ch[n] for n in CONST_F32 + CONST_CAST_BF16}



def build_ln_test(T):
    nc = bass.Bass("TRN2", target_bir_lowering=False)
    x = nc.dram_tensor("x", [D, T], F32, kind="ExternalInput").ap()
    xn_out = nc.dram_tensor("xn", [D, T], BF16, kind="ExternalOutput").ap()
    ch = make_consts_host()
    P = Prog()
    ctxs = []
    ar = Arena(nc, ctxs)
    P.psum = []
    for i in range(8):
        cm = nc.psum_tensor(f"pb{i}", [128, 512], F32)
        P.psum.append(cm.__enter__())
        ctxs.append(cm)
    consts = declare_consts(nc, P, ar, ch)
    emit_layernorm(P, ar, x, xn_out, T, "ln0", (0, 1, 2, 3), consts,
                   xkey=lambda c, tb: ("xin",),
                   xnkey=lambda c, tb: ("xn", c, tb))
    P.finalize(nc, ["lno0", "lno1"])
    for cm in reversed(ctxs):
        cm.__exit__(None, None, None)
    return nc, ch


def emit_pair_flags(P, flags_sh, nonce_sb, idx, dynpar, deps, q="sync"):
    P.dma(q, "fl_" + q,
          lambda E, S: E.dma_start(
              flags_sh[idx:idx + 1, bass.ds(dynpar[q], 1)], nonce_sb[0:1, 1:2]),
          reads=list(deps) + [("nonce_sb",)], writes=[("flagw", idx)])


def emit_pair_poll(P, regsG, flags_sh, idx):
    nonce_g, poll_g, diff_g = regsG

    def poll(E, S):
        for slot in range(2):
            E.load(poll_g, flags_sh[idx:idx + 1, slot:slot + 1])
            E.reg_sub(diff_g, poll_g, nonce_g)
            with E.While(diff_g):
                E.load(poll_g, flags_sh[idx:idx + 1, slot:slot + 1])
                E.reg_sub(diff_g, poll_g, nonce_g)
        return E.nop()
    P.op("gpsimd", poll, reads=[("flagw", idx)],
         writes=[("pairbar", idx)])



def build_program(layers=(0, 1)):
    nc = bass.Bass("TRN2", target_bir_lowering=False)
    ein = lambda n, shp, dt: nc.dram_tensor(n, shp, dt, kind="ExternalInput").ap()

    x_in = ein("x", [D, 2048], F32)
    w0q = ein("w0q", [D, 2048], F32)
    w0k = ein("w0k", [D, 2048], F32)
    w0v = ein("w0v", [D, 2048], F32)
    w0o = ein("w0o", [D, D], F32)
    w0f = ein("w0f", [D, D], F32)
    a0 = ein("a0", [D, 256], F32)
    b0 = ein("b0", [256, D], F32)
    bias0 = ein("bias0", [128, 32], F32)
    cexp = ein("cexp", [128, 2], F32)
    sexp = ein("sexp", [128, 2], F32)
    cosP = ein("cosP", [128, 2048], F32)
    ssP = ein("ssP", [128, 2048], F32)
    w1q = ein("w1q", [D, D], F32)
    w1k = ein("w1k", [D, 512], F32)
    w1v = ein("w1v", [D, 512], F32)
    w1o = ein("w1o", [D, D], F32)
    packed = ein("packed", [512, D], I32)
    scale_exp = ein("scale_exp", [512, D], F32)
    bias1 = ein("bias1", [128, 32], F32)
    qinfo = ein("qinfo", [1, 8], I32)
    cosq = ein("cosq", [128, 1024], F32)
    ssq = ein("ssq", [128, 1024], F32)
    zext = ein("zeroes", [128, 512], F32)

    dram = lambda n, shp, dt: nc.dram_tensor(n, shp, dt).ap()
    shram = lambda n, shp, dt: nc.dram_tensor(n, shp, dt, addr_space="Shared").ap()
    xnloc = dram("xnloc", [D, 2048], BF16)
    xn_sh = shram("xn_sh", [D, 4096], BF16)
    q_d = dram("q_d", [2048, 4096], BF16)
    k_d = dram("k_d", [2048, 4096], BF16)
    v_d = dram("v_d", [2048, 4096], BF16)
    atl = dram("atl", [2048, 4096], BF16)
    at_sh = shram("at_sh", [D, 4096], BF16)
    ato = dram("ato", [D, 2048], BF16)
    g_d = dram("g_d", [D, 2048], F32)
    xa_d = dram("xa_d", [256, 2048], BF16)
    x1loc = dram("x1loc", [D, 2048], F32)
    x1_sh = shram("x1_sh", [D, 4096], F32)
    w1_d = dram("w1_d", [D, D], BF16)
    xn1loc = dram("xn1loc", [D, 2048], BF16)
    xn1_sh = shram("xn1_sh", [D, 4096], BF16)
    k2loc = dram("k2loc", [512, 5120], BF16)
    v2loc = dram("v2loc", [512, 5120], BF16)
    k2_sh = shram("k2_sh", [1024, 5120], BF16)
    v2_sh = shram("v2_sh", [1024, 5120], BF16)
    xnq_d = dram("xnq_d", [D, 1024], BF16)
    q1_d = dram("q1_d", [D, 1024], BF16)
    x1q_d = dram("x1q_d", [D, 1024], F32)
    kw_d = dram("kw_d", [1024, 2048], BF16)
    vw_d = dram("vw_d", [1024, 2048], BF16)
    flags_sh = shram("flags_sh", [8, 2], I32)
    out_e = nc.dram_tensor("out", [D, 1024], F32, kind="ExternalOutput").ap()

    ch = make_consts_host()
    ch["negB"] = np.full((128, 1), -SOFTMAX_B, np.float32)

    P = Prog()
    ctxs = []
    ar = Arena(nc, ctxs)
    P.psum = []
    for i in range(4):
        cm = nc.psum_tensor(f"pb{i}", [128, 1024], F32)
        P.psum.append(cm.__enter__())
        ctxs.append(cm)
    consts = declare_consts(nc, P, ar, ch)
    negB_e = ein("c_negB", [128, 1], F32)
    sbB = ar.alloc("f32", 1)
    P.dma("sync", "consts", lambda E, S: E.dma_start(out=sbB, in_=negB_e[:, :]),
          writes=[("c", "negB")])
    consts["negB"] = sbB
    bias_sb = {0: ar.alloc("f32", 32), 1: ar.alloc("f32", 32)}
    P.dma("sync", "consts", lambda E, S: E.dma_start(out=bias_sb[0], in_=bias0[:, :]),
          writes=[("c", "bias")])
    P.dma("sync", "consts", lambda E, S: E.dma_start(out=bias_sb[1], in_=bias1[:, :]),
          writes=[("c", "bias")])
    ce_sb = ar.alloc("f32", 2)
    se_sb = ar.alloc("f32", 2)
    wv_sb = ar.alloc("f32", 2)
    P.dma("sync", "consts", lambda E, S: E.dma_start(out=ce_sb, in_=cexp[:, :]),
          writes=[("c", "ce")])
    P.dma("sync", "consts", lambda E, S: E.dma_start(out=se_sb, in_=sexp[:, :]),
          writes=[("c", "se")])
    P.op("vector", lambda E, S: E.tensor_tensor(out=wv_sb, in0=ce_sb, in1=se_sb,
                                                op=ALU.mult),
         reads=[("c", "ce"), ("c", "se")], writes=[("c", "wvec")])
    nonce_sb = ar.alloc("i32", 2, parts=1)
    P.dma("sync", "consts", lambda E, S: E.dma_start(
        out=nonce_sb[0:1, 0:2], in_=qinfo[0:1, 6:8]),
        writes=[("nonce_sb",)])

    regS = [nc.sync.alloc_register(name=f"rs{j}") for j in range(4)]
    regC = [nc.scalar.alloc_register(name=f"rc{j}") for j in range(4)]
    regG = [nc.gpsimd.alloc_register(name=f"rg{j}") for j in range(5)]
    dynS, dynC, dynG = {}, {}, {}
    dynpar = {}

    def initregs(E, S):
        E.load(regS[0], qinfo[0:1, 3:4])
        E.load(regS[1], qinfo[0:1, 4:5])
        E.load(regS[2], qinfo[0:1, 5:6])
        ins = E.load(regS[3], qinfo[0:1, 6:7])
        dynS["g0"] = E.snap(regS[0], min_val=0, max_val=1024)
        dynS["g1"] = E.snap(regS[1], min_val=2048, max_val=3072)
        dynS["hrow"] = E.snap(regS[2], min_val=0, max_val=2048)
        dynpar["sync"] = E.snap(regS[3], min_val=0, max_val=1)
        return ins
    P.op("sync", initregs, writes=[("qregs",)])

    def initregs_c(E, S):
        E.load(regC[0], qinfo[0:1, 3:4])
        E.load(regC[1], qinfo[0:1, 4:5])
        E.load(regC[2], qinfo[0:1, 1:2])
        ins = E.load(regC[3], qinfo[0:1, 6:7])
        dynC["g0"] = E.snap(regC[0], min_val=0, max_val=1024)
        dynC["g1"] = E.snap(regC[1], min_val=2048, max_val=3072)
        dynC["kvrow"] = E.snap(regC[2], min_val=0, max_val=512)
        dynpar["scalar"] = E.snap(regC[3], min_val=0, max_val=1)
        return ins
    P.op("scalar", initregs_c, writes=[("qregs_c",)])

    def initregs_g(E, S):
        E.load(regG[3], qinfo[0:1, 0:1])
        ins = E.load(regG[4], qinfo[0:1, 2:3])
        dynG["tok0"] = E.snap(regG[3], min_val=0, max_val=3072)
        dynG["wcol"] = E.snap(regG[4], min_val=0, max_val=3072)
        return ins
    P.op("gpsimd", initregs_g, writes=[("qregs_g",)])

    def initregs_g2(E, S):
        return E.load(regG[0], qinfo[0:1, 7:8])
    P.op("gpsimd", initregs_g2, writes=[("qregs_g2",)])
    regsG = (regG[0], regG[1], regG[2])

    vis0, masks0 = build_visits_l0()
    mask_ext0 = {}
    for idx, kb in enumerate(masks0):
        m = masks0[kb]
        ext = ein(f"c_mk{idx}", [128, m.shape[1]], F32)
        ch[f"c_mk{idx}"] = m.astype(np.float32)
        mask_ext0[kb] = ext
    m1_ext = {}
    for st in range(16):
        m1_ext[st] = ein(f"n1_{st}", [128, 1024], F32)

    banksA = ((3, 0), (3, 1))
    BLK = 2048

    def dyncopy(name, dst_fn, n=1):
        P.dma("sync", name[0], dst_fn, reads=name[1], writes=name[2])

    emit_layernorm(P, ar, x_in, xnloc, BLK, "ln0",
                   ((0, 0), (0, 1), (1, 0), (1, 1)), consts,
                   xkey=lambda c, tb: ("L0", "xsrc"),
                   xnkey=lambda c, tb: ("L0", "xn", c, tb))
    P.barrier()
    P.dma("sync", "qcp",
          lambda E, S: E.dma_start(out=xn_sh[:, bass.ds(dynS["g0"], 1024)],
                                   in_=xnloc[:, 0:1024]),
          reads=[("qregs",)], writes=[("xnsh", 0)])
    P.dma("sync", "qcp2",
          lambda E, S: E.dma_start(out=xn_sh[:, bass.ds(dynS["g1"], 1024)],
                                   in_=xnloc[:, 1024:2048]),
          reads=[("qregs",)], writes=[("xnsh", 1)])
    emit_pair_flags(P, flags_sh, nonce_sb, 0, dynpar,
                    [("xnsh", 0), ("xnsh", 1)], q="sync")
    ar.push()
    epi_xa = emit_scale_epilogue(P, ar, "xa0", xa_d, wv_sb)
    emit_gemm_fm(P, ar, "xa0", a0, 256, xnloc, BLK, banksA, epi_xa,
                 consts, osub=256)
    ar.pop()
    P.barrier()
    emit_pair_poll(P, regsG, flags_sh, 0)
    ar.push()
    epi_q = emit_rope_epilogue(P, ar, "q0", q_d, consts, 0)
    gq = make_gemm_fm_gen(P, ar, "q0", w0q, 2048, xn_sh, 4096, banksA, epi_q,
                          consts, rhs_key=lambda: ("pairbar", 0))
    for _ in gq:
        pass
    ar.pop()
    P.barrier()
    ar.push()
    epi_k = emit_rope_epilogue(P, ar, "k0", k_d, consts, 0)
    gk = make_gemm_fm_gen(P, ar, "k0", w0k, 2048, xn_sh, 4096, banksA, epi_k,
                          consts, rhs_key=lambda: ("pairbar", 0))
    for _ in gk:
        pass
    ar.pop()
    P.barrier()
    emit_gemm_tm_v(P, ar, "v0", w0v, 2048, xn_sh, 4096, banksA, v_d,
                   consts, part_major=True, col0=0,
                   rhs_key=lambda: ("pairbar", 0))
    P.barrier()
    ar.push()
    ffn_gen = make_ffn_gen(P, ar, "f0", w0f, xnloc, BLK, banksA,
                           bias_sb[0], g_d, consts,
                           lora=dict(bf_dram=b0, xa_dram=xa_d),
                           osub=512, nslab=2, slabq="scalar")
    emit_attention2(P, ar, "a0", q_d, k_d, v_d, atl, 4096, 4096, 16, 1,
                    consts, mask_ext0, vis0, co=ffn_gen, pulls=0,
                    nkv_buf=1, co_every=10, nprobs=2, nao=1)
    ar.pop()
    P.barrier()
    P.dma("sync", "qcp",
          lambda E, S: E.dma_start(out=at_sh[bass.ds(dynS["hrow"], 2048), :],
                                   in_=atl[:, :]),
          reads=[("qregs",)], writes=[("atsh",)])
    emit_pair_flags(P, flags_sh, nonce_sb, 1, dynpar, [("atsh",)], q="sync")
    emit_pair_poll(P, regsG, flags_sh, 1)
    P.dma("sync", "qcp",
          lambda E, S: E.dma_start(out=ato[:, 0:1024],
                                   in_=at_sh[:, bass.ds(dynS["g0"], 1024)]),
          reads=[("pairbar", 1)], writes=[("ato", 0)])
    P.dma("sync", "qcp2",
          lambda E, S: E.dma_start(out=ato[:, 1024:2048],
                                   in_=at_sh[:, bass.ds(dynS["g1"], 1024)]),
          reads=[("pairbar", 1)], writes=[("ato", 1)])
    P.barrier()
    ar.push()
    epi_r = emit_resid_epilogue(P, ar, "r0", g_d, x_in, x1loc,
                                gkey=lambda og, tb: ("L0", "g"),
                                xkey=lambda og, tb: ("L0", "xsrc"),
                                out_key=lambda og, tb: ("L0", "xo", og, tb),
                                bias_sb=bias_sb[0])
    emit_gemm_fm(P, ar, "r0", w0o, D, ato, BLK, banksA, epi_r, consts)
    ar.pop()
    P.barrier()
    P.dma("scalar", "qcpc",
          lambda E, S: E.dma_start(out=x1_sh[:, bass.ds(dynC["g0"], 1024)],
                                   in_=x1loc[:, 0:1024]),
          reads=[("qregs_c",)], writes=[("x1sh", 0)])
    P.dma("scalar", "qcpc2",
          lambda E, S: E.dma_start(out=x1_sh[:, bass.ds(dynC["g1"], 1024)],
                                   in_=x1loc[:, 1024:2048]),
          reads=[("qregs_c",)], writes=[("x1sh", 1)])
    emit_pair_flags(P, flags_sh, nonce_sb, 2, dynpar,
                    [("x1sh", 0), ("x1sh", 1)], q="scalar")

    emit_layernorm(P, ar, x1loc, xn1loc, BLK, "ln1",
                   ((0, 0), (0, 1), (1, 0), (1, 1)), consts,
                   xkey=lambda c, tb: ("L1", "xsrc"),
                   xnkey=lambda c, tb: ("L1", "xn", c, tb))
    P.barrier()
    P.dma("scalar", "qcpc",
          lambda E, S: E.dma_start(out=xn1_sh[:, bass.ds(dynC["g0"], 1024)],
                                   in_=xn1loc[:, 0:1024]),
          reads=[("qregs_c",)], writes=[("xn1sh", 0)])
    P.dma("scalar", "qcpc2",
          lambda E, S: E.dma_start(out=xn1_sh[:, bass.ds(dynC["g1"], 1024)],
                                   in_=xn1loc[:, 1024:2048]),
          reads=[("qregs_c",)], writes=[("xn1sh", 1)])
    emit_pair_flags(P, flags_sh, nonce_sb, 3, dynpar,
                    [("xn1sh", 0), ("xn1sh", 1)], q="scalar")
    emit_pair_poll(P, regsG, flags_sh, 3)
    emit_pair_poll(P, regsG, flags_sh, 2)
    P.dma("gpsimd", "qcpg",
          lambda E, S: E.dma_start(out=xnq_d[:, :],
                                   in_=xn1_sh[:, bass.ds(dynG["tok0"], 1024)]),
          reads=[("pairbar", 3), ("qregs_g",)], writes=[("xnq",)])
    P.dma("gpsimd", "qcpg2",
          lambda E, S: E.dma_start(out=x1q_d[:, :],
                                   in_=x1_sh[:, bass.ds(dynG["tok0"], 1024)]),
          reads=[("pairbar", 2), ("qregs_g",)], writes=[("x1q",)])
    emit_dequant(P, ar, "dq", packed, scale_exp, w1_d, consts)
    P.barrier()
    zt = ar.alloc("bf16", 512)
    P.dma("gpsimd", "constsb",
          lambda E, S: E.dma_start(out=zt, in_=zext[:, :]),
          writes=[("zt",)])
    for rc in range(4):
        for cc in range(2):
            P.dma("sync", "padz",
                  lambda E, S, rc=rc, cc=cc: E.dma_start(
                      out=k2loc[rc * 128:(rc + 1) * 128, cc * 512:(cc + 1) * 512],
                      in_=zt),
                  reads=[("zt",)], writes=[("kpad", rc, cc)])
            P.dma("sync", "padz",
                  lambda E, S, rc=rc, cc=cc: E.dma_start(
                      out=v2loc[rc * 128:(rc + 1) * 128, cc * 512:(cc + 1) * 512],
                      in_=zt),
                  reads=[("zt",)], writes=[("vpad", rc, cc)])
    ar.push()
    epi_k1 = emit_rope_epilogue(P, ar, "k1", k2loc[:, 1024:5120], consts, 0)
    gk1 = make_gemm_fm_gen(P, ar, "k1", w1k, 512, xn1_sh, 4096, banksA, epi_k1,
                           consts, osub=512, rhs_key=lambda: ("pairbar", 3))
    for _ in gk1:
        pass
    ar.pop()
    P.barrier()
    emit_gemm_tm_v(P, ar, "v1", w1v, 512, xn1_sh, 4096, banksA,
                   v2loc, consts, part_major=True, col0=1024,
                   rhs_key=lambda: ("pairbar", 3))
    P.barrier()
    P.dma("scalar", "qcpc",
          lambda E, S: E.dma_start(
              out=k2_sh[bass.ds(dynC["kvrow"], 512), :], in_=k2loc[:, :]),
          reads=[("kpad", 3, 1), ("qregs_c",)], writes=[("k2sh",)])
    P.dma("scalar", "qcpc2",
          lambda E, S: E.dma_start(
              out=v2_sh[bass.ds(dynC["kvrow"], 512), :], in_=v2loc[:, :]),
          reads=[("vpad", 3, 1), ("qregs_c",)], writes=[("v2sh",)])
    emit_pair_flags(P, flags_sh, nonce_sb, 4, dynpar,
                    [("k2sh",), ("v2sh",)], q="scalar")
    ar.push()
    consts_q = dict(consts)
    cq_sb = ar.alloc("f32", 1024)
    sq_sb = ar.alloc("f32", 1024)
    P.dma("sync", "consts", lambda E, S: E.dma_start(out=cq_sb, in_=cosq[:, :]),
          writes=[("c", "cos_t")])
    P.dma("sync", "consts", lambda E, S: E.dma_start(out=sq_sb, in_=ssq[:, :]),
          writes=[("c", "sinsign")])
    consts_q["cos_t"] = cq_sb
    consts_q["sinsign"] = sq_sb
    epi_q1 = emit_rope_epilogue(P, ar, "q1", q1_d, consts_q, 0)
    gq1 = make_gemm_fm_gen(P, ar, "q1", w1q, D, xnq_d, 1024, banksA, epi_q1,
                           consts, rhs_key=lambda: ("xnq",))
    for _ in gq1:
        pass
    ar.pop()
    P.barrier()
    emit_pair_poll(P, regsG, flags_sh, 4)
    P.dma("gpsimd", "qcpg",
          lambda E, S: E.dma_start(out=kw_d[:, :],
                                   in_=k2_sh[:, bass.ds(dynG["wcol"], 2048)]),
          reads=[("pairbar", 4), ("qregs_g",)], writes=[("kw",)])
    P.dma("gpsimd", "qcpg2",
          lambda E, S: E.dma_start(out=vw_d[:, :],
                                   in_=v2_sh[:, bass.ds(dynG["wcol"], 2048)]),
          reads=[("pairbar", 4), ("qregs_g",)], writes=[("vw",)])
    P.barrier()
    ar.push()
    ffn1_gen = make_ffn_gen(P, ar, "f1", w1_d, xnq_d, 1024, banksA,
                            bias_sb[1], g_d, consts, w_is_f32=False,
                            osub=512, nslab=1, slabq="scalar")
    vis1 = {0: [(st, st) for st in range(16)]}
    emit_attention2(P, ar, "a1", q1_d, kw_d, vw_d, ato, 1024, 2048, 32, 4,
                    consts, m1_ext, vis1, co=ffn1_gen, pulls=0,
                    nkv_buf=1, co_every=8)
    ar.pop()
    P.barrier()
    ar.push()
    epi_r1 = emit_resid_epilogue(P, ar, "r1", g_d, x1q_d, out_e,
                                 gkey=lambda og, tb: ("L1", "g"),
                                 xkey=lambda og, tb: ("x1q",),
                                 out_key=lambda og, tb: ("L1", "xo", og, tb),
                                 bias_sb=bias_sb[1])
    emit_gemm_fm(P, ar, "r1", w1o, D, ato, 1024, banksA, epi_r1, consts)
    ar.pop()

    P.finalize(nc, ["rr0", "rr1"])
    for cm in reversed(ctxs):
        cm.__exit__(None, None, None)
    return nc, ch



def stage_core_inputs(inputs, batch, ch, quarter=0, nonce=1):
    f = np.ascontiguousarray
    p = quarter % 2
    blocks = [p, 3 - p]
    x = inputs["L0_attn_x"][batch]
    xt = x.T
    xloc = np.concatenate([xt[:, b * 1024:(b + 1) * 1024] for b in blocks], 1)
    cos_t, sinsign = ch["cos_t"], ch["sinsign"]
    tok0 = quarter * 1024
    im = {
        "x": f(xloc).astype(np.float32, copy=False),
        "w0q": f(inputs["L0_attn_Wq"].T[:, p * 2048:(p + 1) * 2048]),
        "w0k": f(inputs["L0_attn_Wk"].T[:, p * 2048:(p + 1) * 2048]),
        "w0v": f(inputs["L0_attn_Wv"].T[:, p * 2048:(p + 1) * 2048]),
        "w0o": f(inputs["L0_attn_Wo"].T),
        "w0f": f(inputs["L0_ffn_W"].T),
        "a0": f(inputs["L0_ffn_A"].reshape(256, D).T),
        "b0": f(inputs["L0_ffn_Bf"].transpose(0, 2, 1).reshape(256, D)),
        "bias0": f(inputs["L0_ffn_bias"].reshape(32, 128).T),
        "cexp": f(inputs["L0_ffn_coeffs"][(np.arange(256) // 16)].reshape(2, 128).T),
        "sexp": f(inputs["L0_ffn_scales"][(np.arange(256) // 16)].reshape(2, 128).T),
        "cosP": f(np.concatenate([cos_t[:, b * 1024:(b + 1) * 1024] for b in blocks], 1)),
        "ssP": f(np.concatenate([sinsign[:, b * 1024:(b + 1) * 1024] for b in blocks], 1)),
        "w1q": f(inputs["L1_attn_wq"]),
        "w1k": f(inputs["L1_attn_wk"][:, p * 512:(p + 1) * 512]),
        "w1v": f(inputs["L1_attn_wv"][:, p * 512:(p + 1) * 512]),
        "w1o": f(inputs["L1_attn_wo"]),
        "packed": f(inputs["L1_ffn_packed_weight"].T),
        "scale_exp": f(inputs["L1_ffn_scale"].T[(np.arange(512) // 16)]),
        "bias1": f(inputs["L1_ffn_bias"].reshape(32, 128).T),
    }
    im.update(const_in_map(ch))
    im["c_negB"] = ch["negB"]
    for k, v in ch.items():
        if k.startswith("c_mk"):
            im[k] = v
    im["zeroes"] = np.zeros((128, 512), np.float32)
    im["qinfo"] = np.array([[tok0, p * 512, tok0, blocks[0] * 1024,
                             blocks[1] * 1024, p * 2048, p, nonce]], np.int32)
    im["cosq"] = f(cos_t[:, tok0:tok0 + 1024])
    im["ssq"] = f(sinsign[:, tok0:tok0 + 1024])
    for st, m in l1_mask_arrays2(quarter).items():
        im[f"n1_{st}"] = m
    return im


N_LAYER_LAUNCHES = 1


def kernel(**inputs):
    nonce = int(np.random.randint(1, 2 ** 30))
    nc, ch = build_program((0, 1))
    in_maps = [stage_core_inputs(inputs, c // 4, ch, c % 4, nonce)
               for c in range(8)]
    res = run_bass_kernel_spmd(nc, in_maps, core_ids=list(range(8)))
    outs = []
    for b in range(2):
        qs = [res.results[b * 4 + q]["out"].T for q in range(4)]
        outs.append(np.ascontiguousarray(np.concatenate(qs, 0)))
    return np.stack(outs, 0)
